# version 22
# baseline (speedup 1.0000x reference)
import math
import numpy as np
import concourse.bass as bass
import concourse.mybir as mybir
from concourse.bass_utils import run_bass_kernel_spmd

F32 = mybir.dt.float32
BF16 = mybir.dt.bfloat16
I32 = mybir.dt.int32
AF = mybir.ActivationFunctionType
ALU = mybir.AluOpType
AX = mybir.AxisListType

MAGIC = float(np.float32(1.5 * 2 ** 23))
EPS = 1e-5
LAMBDA_INIT = float(np.float32(0.8 - 0.6 * math.exp(-0.3 * 1)))
ALPHA = float(np.float32(128 ** -0.5))
NEG = -1e30

FULL = dict(D=2048, F=5504, TO=2048, TC=2048, PH=8, NW=512,
            ASC=2048, BSC=512, QC=256)

ENGS = ("sync", "vector", "scalar", "tensor", "gpsimd")
_SEMS = ("ls", "lv", "lc", "lt", "lq",
         "ga", "gb", "gc", "gd", "ge",
         "ha", "hb", "hc", "hd", "he", "hf", "hg",
         "qa", "qb", "qc", "qd", "qe", "gx", "gy", "wga", "wgb", "rb",
         "ha2", "ls2", "he2", "qa2", "ob2", "od2",
         "aa", "ab", "ac", "ad", "ae", "af", "ag", "ao", "aq",
         "oa", "ob", "oc", "od",
         "fs", "fv", "fc", "ft", "fg")
_FENCE = {"sync": "fs", "vector": "fv", "scalar": "fc",
          "tensor": "ft", "gpsimd": "fg"}


class E:

    def __init__(self, me, eng, sems):
        self.me = me
        self.eng = eng
        self.sems = sems
        self.cnt = {k: 0 for k in sems}
        self.base = {k: 0 for k in sems}
        self.last = None
        self.last_compute = None
        self.dirty = {x: False for x in ENGS}
        self.cdirty = {x: False for x in ENGS}
        self.dma_sems = {x: set() for x in ENGS}
        self.last_op_sem = {x: None for x in ENGS}

    def op(self, owner, f, sem=None, amt=1):
        if sem is not None:
            self.cnt[sem] += amt
            self.last_op_sem[owner] = (sem, self.cnt[sem])
        else:
            self.last_op_sem[owner] = None
        self.dirty[owner] = True
        self.cdirty[owner] = True
        if self.me == owner:
            ins = f(self.eng)
            self.last = ins
            self.last_compute = ins
            if sem is not None:
                ins.then_inc(self.sems[sem], amt)
            return ins

    def inc_last(self, owner, sem, amt=1):
        self.cnt[sem] += amt
        self.last_op_sem[owner] = (sem, self.cnt[sem])
        if self.me == owner:
            self.last.then_inc(self.sems[sem], amt)

    def dma(self, owner, sem, out, in_):
        self.cnt[sem] += 16
        self.dirty[owner] = True
        self.dma_sems[owner].add(sem)
        if self.me == owner:
            ins = self.eng.dma_start(out=out, in_=in_)
            self.last = ins
            ins.then_inc(self.sems[sem], 16)
            return ins

    def wait(self, owner, sem, local=None):
        if local is None:
            val = self.cnt[sem]
            if val == self.base[sem]:
                return
        else:
            if local <= 0:
                return
            val = self.base[sem] + local
            assert val <= self.cnt[sem], (sem, val, self.cnt[sem])
        if self.me == owner:
            self.eng.wait_ge(self.sems[sem], val)

    def loc(self, sem):
        return self.cnt[sem] - self.base[sem]

    def barrier(self):
        used = sorted(set().union(*self.dma_sems.values()))
        for x in ENGS:
            for s in used:
                self.wait(x, s)
            self.dma_sems[x] = set()
        fences = []
        for x in ENGS:
            if self.cdirty[x]:
                los = self.last_op_sem[x]
                if los is not None:
                    fences.append(los)
                else:
                    fs = _FENCE[x]
                    self.cnt[fs] += 1
                    fences.append((fs, self.cnt[fs]))
                    if self.me == x:
                        assert self.last_compute is not None
                        self.last_compute.then_inc(self.sems[fs], 1)
            self.cdirty[x] = False
            self.dirty[x] = False
            self.last_op_sem[x] = None
        for x in ENGS:
            for (s, v) in fences:
                if self.me == x:
                    self.eng.wait_ge(self.sems[s], v)
        self.base = dict(self.cnt)
        self.last = None
        self.last_compute = None


def build(cfg):
    D, F, TO, TC = cfg["D"], cfg["F"], cfg["TO"], cfg["TC"]
    PH, NW = cfg["PH"], cfg["NW"]
    ASC, BSC, QC = cfg["ASC"], cfg["BSC"], cfg["QC"]
    KVPH = PH // 2
    QF, KF, VF = PH * 256, KVPH * 256, KVPH * 256
    QKVF = QF + KF + VF
    DT, FT = D // 128, F // 128
    TA = TC + TO
    KT = TA // 128
    QS = QC // 128
    VEXT = 257
    FCH = [(i * NW, min(NW, F - i * NW)) for i in range((F + NW - 1) // NW)]
    DCH = [(i * NW, min(NW, D - i * NW)) for i in range((D + NW - 1) // NW)]
    NFC = len(FCH)
    G3 = (FT + 2) // 3
    W3 = [(i * G3, min(G3, FT - i * G3)) for i in range(3) if i * G3 < FT]
    NP = len(W3)
    NBK = (DT + 3) // 4

    nc = bass.Bass("TRN2", target_bir_lowering=False)

    def din(name, shape, dt=F32):
        return nc.dram_tensor(name, shape, dt, kind="ExternalInput").ap()

    x_own = din("x_own", [TO, D])
    x_ctx = din("x_ctx", [TC, D])
    SPLIT = bool(cfg.get("split"))
    KWG = (FT + 7) // 8
    KWD = DT // 8
    if SPLIT:
        wg_d = din("wg_slc", [KWG * 128, D])
        wu_d = din("wu_slc", [KWG * 128, D])
        wd_d = din("wd_slc", [KWD * 128, F])
    else:
        wg_d = din("wg", [F, D])
        wu_d = din("wu", [F, D])
        wd_d = din("wd", [D, F])
    wqkvT_d = din("wqkvT", [D, QKVF])
    woT_d = din("woT", [QF, D])
    bqkv_d = din("bqkv", [1, QKVF])
    bo_d = din("bo", [1, D])
    lrows_d = din("lrows_in", [4, 128])
    sublnw_d = din("sublnw_in", [1, 256])
    identf_d = din("identf_in", [128, 128])
    bandmask_d = din("bandmask_in", [QS * 128, QC])
    ctxbias_d = din("ctxbias", [1, 1])

    out_d = nc.dram_tensor("out", [TO, D], F32, kind="ExternalOutput").ap()

    def scr(name, shape, dt):
        return nc.dram_tensor(name, shape, dt, kind="Internal").ap()

    FPAD = (KWG * 8 * 128 + 128) if SPLIT else F
    wgq_sc = scr("wgq_sc", [D, FPAD], BF16)
    wuq_sc = scr("wuq_sc", [D, FPAD], BF16)
    wdq_sc = scr("wdq_sc", [F, D], BF16)
    if SPLIT:
        wgq_loc = scr("wgq_loc", [D, KWG * 128], BF16)
        wuq_loc = scr("wuq_loc", [D, KWG * 128], BF16)
        wdq_loc = scr("wdq_loc", [F, KWD * 128], BF16)
    wqkv_bsc = scr("wqkv_bsc", [D, QKVF], BF16)
    woT_bsc = scr("woT_bsc", [QF, D], BF16)
    h_sc = scr("h_sc", [TA, F], F32)
    x2_sc = scr("x2_sc", [TA, D], F32)
    x3_sc = scr("x3_sc", [TO, D], F32)
    qk_sc = scr("qk_sc", [QF + KF, TA], BF16)
    v_sc = scr("v_sc", [TA, VF], BF16)
    dump = {}
    if cfg.get("dump"):
        for nm, shape, dt in [("o_x2", [TA, D], F32), ("o_qk", [QF + KF, TA], BF16),
                              ("o_v", [TA, VF], BF16), ("o_x3", [TO, D], F32),
                              ("o_h", [TA, F], F32),
                              ("o_dxq", [((TA + TO) // cfg["ASC"] + 4) * 128,
                                         (D // 128) * cfg["ASC"]], BF16),
                              ("o_dst", [((TA + TO) // cfg["ASC"] + 4) * 128,
                                         cfg["ASC"] // 128], F32)]:
            dump[nm] = nc.dram_tensor(nm, shape, dt, kind="ExternalOutput").ap()
    dbg_xq = (scr("dbg_xq", [((TA + TO) // cfg["ASC"] + 4) * 128, (D // 128) * cfg["ASC"]], BF16)
              if cfg.get("dbgx") else None)
    dbg_st = (scr("dbg_st", [((TA + TO) // cfg["ASC"] + 4) * 128, cfg["ASC"] // 128], F32)
              if cfg.get("dbgx") else None)

    import contextlib
    es = contextlib.ExitStack()

    def sb(name, p, f, dt=F32):
        return es.enter_context(nc.sbuf_tensor(name, [p, f], dt))

    with es:
        big_a = sb("big_a", 128, max(DT * ASC, 2 * PH * TO), BF16)
        big_b = sb("big_b", 128, max(FT * BSC, 2 * TA + 2 * TO + KT * VEXT, 20480), BF16)
        wbuf = sb("wbuf", 128,
                  max(2 * DT * NW, 2 * G3 * NW, 2 * 2 * PH * NW), BF16)
        xbuf = sb("xbuf", 128, max(2 * D, 4096), F32)
        hbuf = sb("hbuf", 128, max(D, 4 * NW), F32)
        ebuf = sb("ebuf", 128, max(2 * max(NW, QC, D // 2), 4 * 512), BF16)
        i32b = sb("i32b", 128, NW, I32)
        i32c = sb("i32c", 128, NW, I32)
        obuf = sb("obuf", 128, 6 * 256, F32)
        gbuf = sb("gbuf", 128, 2 * 128, F32)
        wqb = sb("wqb", 128, 2 * 128, BF16)
        wtp = sb("wtp", 128, 2 * 128, BF16)
        identf = sb("identf", 128, 128, F32)
        identb = sb("identb", 128, 128, BF16)
        bandm = sb("bandm", 128, QS * QC, F32)
        swtile = sb("swtile", 128, 256, F32)
        lamncol = sb("lamncol", 128, 1, F32)
        biascol = sb("biascol", 128, 6, F32)
        mtab = sb("mtab", 128, 32, F32)
        onescol = sb("onescol", 128, 1, BF16)
        onescf = sb("onescf", 128, 1, F32)
        ones1 = sb("ones1", 1, 128, BF16)
        onesrow = sb("onesrow", 1, NW, BF16)
        bqkvrow = sb("bqkvrow", 1, QKVF, BF16)
        borow = sb("borow", 1, D, BF16)
        lrow = sb("lrow", 1, 512, F32)
        sublnw = sb("sublnw", 1, 256, F32)
        srow = sb("srow", 1, 128, F32)
        bn6 = sb("bn6", 128, 6 * max(1, (D + 511) // 512), F32)
        mv = sb("mv", 128, 2, F32)
        rrc = sb("rrc", 128, 2, F32)
        tmpc = sb("tmpc", 128, 8, F32)
        sinv_a = sb("sinv_a", 128, ASC // 128, F32)
        hmax_a = sb("hmax_a", 128, ASC // 128, F32)
        ch_a = sb("ch_a", 128, ASC // 128, F32)
        hs_a = sb("hs_a", 128, ASC // 128, F32)
        wsc = sb("wsc", 128, 4, F32)
        epsc = sb("epsc", 128, 4, F32)
        ps = [es.enter_context(nc.psum_tensor("p%d" % i, [128, 512], F32))
              for i in range(8)]

        def psbf(i):
            return ps[i][:].bitcast(BF16)
        sems = {s: es.enter_context(nc.semaphore(s)) for s in _SEMS}

        dbg_slot = [0]

        def x_in_tile(t):
            r = t * 128
            if r < TC:
                return x_ctx[r:r + 128, :]
            return x_own[r - TC:r - TC + 128, :]

        def p0_consts(e: E):
            e.dma("sync", "ls", identf[:], identf_d)
            e.dma("sync", "ls",
                  bandm[:, 0:QS * QC].rearrange("p (a q) -> p a q", a=QS),
                  bandmask_d.rearrange("(a p) q -> p a q", a=QS))
            e.dma("sync", "ls", lrow[:], lrows_d.rearrange("a b -> (a b)").unsqueeze(0))
            e.dma("sync", "ls", sublnw[:], sublnw_d)
            e.dma("sync", "ls", srow[0:1, 8:9], ctxbias_d)
            for (dst, srcd, L) in ((bqkvrow, bqkv_d, QKVF), (borow, bo_d, D)):
                for c0 in range(0, L, D):
                    w = min(D, L - c0)
                    e.wait("sync", "lv")
                    e.dma("sync", "ls", xbuf[0:1, 0:w], srcd[0:1, c0:c0 + w])
                    e.wait("vector", "ls")
                    e.op("vector", lambda g, dst=dst, c0=c0, w=w: g.tensor_copy(
                        dst[0:1, c0:c0 + w], xbuf[0:1, 0:w]), sem="lv")
            e.wait("vector", "ls")
            e.op("vector", lambda g: g.memset(epsc[:, 0:1], EPS))
            e.op("vector", lambda g: g.memset(epsc[:, 1:2], 1e-38))
            e.op("vector", lambda g: g.memset(epsc[:, 2:3], MAGIC))
            e.op("vector", lambda g: g.memset(epsc[:, 3:4], -MAGIC))
            e.op("vector", lambda g: g.memset(onescol[:], 1.0))
            e.op("vector", lambda g: g.memset(onescf[:], 1.0))
            e.op("vector", lambda g: g.memset(ones1[:], 1.0))
            e.op("vector", lambda g: g.memset(onesrow[:], 1.0))
            e.op("vector", lambda g: g.tensor_copy(identb[:], identf[:]))
            e.op("vector", lambda g: g.tensor_mul(lrow[0:1, 0:128],
                                                  lrow[0:1, 0:128],
                                                  lrow[0:1, 128:256]))
            e.op("vector", lambda g: g.tensor_mul(lrow[0:1, 256:384],
                                                  lrow[0:1, 256:384],
                                                  lrow[0:1, 384:512]), sem="lv")
            e.wait("vector", "lv")
            e.op("vector", lambda g: g.reduce_sum(srow[0:1, 0:1], lrow[0:1, 0:128], AX.X))
            e.op("vector", lambda g: g.reduce_sum(srow[0:1, 1:2], lrow[0:1, 256:384], AX.X),
                 sem="lv")
            e.wait("scalar", "lv")
            e.op("scalar", lambda g: g.activation(srow[0:1, 2:4], srow[0:1, 0:2],
                                                  AF.Exp), sem="lc")
            e.wait("vector", "lc")
            e.op("vector", lambda g: g.tensor_sub(srow[0:1, 4:5], srow[0:1, 2:3],
                                                  srow[0:1, 3:4]), sem="lv")
            e.wait("vector", "lv")
            e.op("vector", lambda g: g.tensor_scalar(srow[0:1, 4:5], srow[0:1, 4:5],
                                                     LAMBDA_INIT, -1.0,
                                                     ALU.add, ALU.mult))
            e.op("vector", lambda g: g.tensor_scalar_mul(sublnw[0:1, :],
                                                         sublnw[0:1, :],
                                                         1.0 - LAMBDA_INIT), sem="lv")
            e.wait("vector", "lv")
            e.op("vector", lambda g: g.tensor_copy(ebuf[0:1, 4:5], srow[0:1, 4:5]))
            e.op("vector", lambda g: g.tensor_copy(ebuf[0:1, 8:9], srow[0:1, 8:9]))
            e.op("vector", lambda g: g.tensor_copy(ebuf[0:1, 16:16 + 256],
                                                   sublnw[0:1, :]), sem="lv")
            e.wait("tensor", "lv")
            e.op("tensor", lambda g: g.matmul(ps[0][:, 0:1], ones1[:], ebuf[0:1, 4:5],
                                              start=True, stop=True))
            e.op("tensor", lambda g: g.matmul(ps[0][:, 4:5], ones1[:], ebuf[0:1, 8:9],
                                              start=True, stop=True))
            e.op("tensor", lambda g: g.matmul(ps[1][:, 0:256], ones1[:],
                                              ebuf[0:1, 16:16 + 256],
                                              start=True, stop=True), sem="lt")
            e.wait("vector", "lt")
            e.op("vector", lambda g: g.tensor_copy(lamncol[:], ps[0][:, 0:1]))
            e.op("vector", lambda g: g.tensor_copy(swtile[:], ps[1][:, 0:256]))
            e.op("vector", lambda g: g.tensor_copy(biascol[:, 4:5], ps[0][:, 4:5]),
                 sem="lv")
            e.barrier()

        def p1_weights(e: E):
            def quant(w_in, w_out, O, I):
                G = I // 128
                OT = O // 128
                wov = w_out.rearrange("(g p) c -> g p c", g=G)
                CH = [(c * 16, min(16, G - c * 16)) for c in range((G + 15) // 16)]
                i32u = big_b[:, 0:8192].bitcast(I32)
                dbw = big_b[:, 8192:16384].bitcast(F32)
                wqrow = big_b[:, 16384:20480]
                for rj in range(OT):
                    for (g0, gc) in CH:
                        W = gc * 128
                        j = (e.loc("ga") + e.loc("gx")) // 16
                        sl = j % 2
                        veng = "vector"
                        qsem = "gb"
                        lsem = "ga" if sl == 0 else "gx"
                        e.wait("sync", qsem, 6 * (j - 1))
                        e.wait("sync", "lv", j - 1)
                        e.dma("sync", lsem, xbuf[:, 2048 * sl:2048 * sl + W],
                              w_in[128 * rj:128 * rj + 128,
                                   128 * g0:128 * g0 + W])
                        wr = xbuf[:, 2048 * sl:2048 * sl + W]
                        wv = wr.rearrange("p (g c) -> p g c", g=gc)
                        ui = i32u[:, 2048 * sl:2048 * sl + W]
                        dd = dbw[:, 2048 * sl:2048 * sl + W]
                        ci32 = dd.bitcast(I32)
                        uf = ui.bitcast(F32)
                        sb_ = gbuf[:, 128 * sl:128 * sl + 96]
                        e.wait("vector", lsem, 16 * (j // 2 + 1))
                        e.op("vector", lambda g, wv=wv, sb_=sb_, gc=gc:
                             g.tensor_reduce(sb_[:, 0:gc], wv, AX.X, ALU.add,
                                             apply_absolute_value=True),
                             sem="ag")
                        e.wait("vector", "ag")
                        e.op("vector", lambda g, sb_=sb_, gc=gc: g.tensor_scalar(
                            sb_[:, 32:32 + gc], sb_[:, 0:gc], 1.0 / 128.0,
                            EPS, ALU.mult, ALU.add), sem="ag")
                        e.wait("vector", "ag")
                        e.op("vector", lambda g, sb_=sb_, gc=gc: g.reciprocal(
                            sb_[:, 64:64 + gc], sb_[:, 32:32 + gc]), sem="ag")
                        e.wait(veng, "ag", 3 * (j + 1))
                        for gi_ in range(gc):
                            e.op(veng, lambda g, gi_=gi_, wv=wv, sb_=sb_:
                                 g.tensor_scalar(
                                     wv[:, gi_, :], wv[:, gi_, :],
                                     sb_[:, 64 + gi_:64 + gi_ + 1], 1.0,
                                     ALU.mult, ALU.min),
                                 sem=(qsem if gi_ == gc - 1 else None))
                        e.wait(veng, qsem)
                        e.op(veng, lambda g, wr=wr: g.tensor_scalar(
                            wr, wr, -1.0, MAGIC, ALU.max, ALU.add), sem=qsem)
                        e.wait(veng, qsem)
                        e.op(veng, lambda g, ui=ui, wr=wr:
                             g.tensor_single_scalar(
                                 ui, wr, MAGIC, ALU.subtract), sem=qsem)
                        e.wait(veng, qsem)
                        e.op(veng, lambda g, dd=dd, wr=wr, ui=ui:
                             g.scalar_tensor_tensor(
                                 dd, wr, -MAGIC, ui,
                                 ALU.add, ALU.subtract), sem=qsem)
                        e.wait(veng, qsem)
                        e.op(veng, lambda g, ci32=ci32, dd=dd:
                             g.tensor_single_scalar(
                                 ci32, dd, 2.0, ALU.mult), sem=qsem)
                        e.wait(veng, qsem)
                        e.op(veng, lambda g, uf=uf, ui=ui, ci32=ci32:
                             g.tensor_add(uf, ui, ci32), sem=qsem)
                        e.wait("scalar", qsem, 6 * (j + 1))
                        e.wait("scalar", "gc")
                        qfv = uf.rearrange("p (g c) -> p g c", g=gc)
                        qv = wqrow[:, 2048 * sl:2048 * sl + W].rearrange(
                            "p (g c) -> p g c", g=gc)
                        for gi_ in range(gc):
                            e.op("scalar", lambda g, gi_=gi_, qfv=qfv, qv=qv,
                                 sb_=sb_:
                                 g.activation(
                                     qv[:, gi_, :], qfv[:, gi_, :], AF.Copy,
                                     scale=sb_[:, 32 + gi_:32 + gi_ + 1]),
                                 sem=("lv" if gi_ == gc - 1 else None))
                        e.wait("tensor", "lv", j + 1)
                        nbk = (gc + 3) // 4
                        for bk in range(nbk):
                            nblk = min(4, gc - 4 * bk)
                            bidx = e.loc("gc")
                            e.wait("tensor", "gd", bidx - 1)
                            for bi in range(nblk):
                                e.op("tensor", lambda g, sl=sl, bk=bk, bi=bi,
                                     bidx=bidx:
                                     g.transpose(
                                         psbf(4 + (bidx % 2))[:, 128 * bi:128 * bi + 128],
                                         wqrow[:, 2048 * sl + 512 * bk + 128 * bi:
                                              2048 * sl + 512 * bk + 128 * bi + 128],
                                         identb[:]),
                                     sem=("gc" if bi == nblk - 1 else None))
                            e.wait("scalar", "gc", bidx + 1)
                            esl2 = bidx % 2
                            ssem = "ge" if esl2 == 0 else "gy"
                            e.wait("scalar", ssem, 16 * (e.loc("gd") // 2))
                            e.op("scalar", lambda g, bidx=bidx, nblk=nblk,
                                 esl2=esl2:
                                 g.activation(
                                     ebuf[:, NW * esl2:NW * esl2 + 128 * nblk],
                                     psbf(4 + (bidx % 2))[:, 0:128 * nblk],
                                     AF.Copy), sem="gd")
                            e.wait("gpsimd", "gd", e.loc("gd"))
                            e.dma("gpsimd", ssem,
                                  wov[g0 + 4 * bk:g0 + 4 * bk + nblk, :,
                                      128 * rj:128 * rj + 128]
                                  .transpose([1, 0, 2]),
                                  ebuf[:, NW * esl2:NW * esl2 + 128 * nblk]
                                  .rearrange("p (b c) -> p b c", b=nblk))
            if SPLIT:
                quant(wg_d, wgq_loc, KWG * 128, D)
                quant(wu_d, wuq_loc, KWG * 128, D)
                quant(wd_d, wdq_loc, KWD * 128, F)
            else:
                quant(wg_d, wgq_sc, F, D)
                quant(wu_d, wuq_sc, F, D)
                quant(wd_d, wdq_sc, D, F)
            e.barrier()
            if SPLIT:
                for (loc, glob, pat, kk) in (
                        (wgq_loc, wgq_sc, "R d (k c) -> S d (k R c)", KWG),
                        (wuq_loc, wuq_sc, "R d (k c) -> S d (k R c)", KWG),
                        (wdq_loc, wdq_sc, "R f (k c) -> S f (R k c)", KWD)):
                    R = loc.shape[0]
                    gw = kk * 8 * 128
                    for r0 in range(0, R, 128):
                        e.cnt["rb"] += 16
                        if e.me == "gpsimd":
                            e.eng.remote_dma_broadcast_einops(
                                glob[r0:r0 + 128, 0:gw],
                                loc[r0:r0 + 128, :], sems["rb"],
                                None, repeat=pat, sizes={"k": kk, "c": 128})
                e.wait("sync", "rb")

            def cast(w_in, w_out, R, C):
                chs = [(i * NW, min(NW, C - i * NW))
                       for i in range((C + NW - 1) // NW)]
                for r in range(R // 128):
                    for ji, (c0, cw) in enumerate(chs):
                        j = e.loc("ha") // 16 + e.loc("ha2") // 16
                        sl = j % 2
                        e.wait("sync", "hb", j - 1)
                        lsem = "ha" if sl == 0 else "ha2"
                        e.dma("sync", lsem, hbuf[:, NW * sl:NW * sl + cw],
                              w_in[128 * r:128 * r + 128, c0:c0 + cw])
                        ssem = "qd" if sl == 0 else "qe"
                        e.wait("vector", lsem, 16 * (j // 2 + 1))
                        e.wait("vector", ssem, 16 * (j // 2))
                        e.op("vector", lambda g, sl=sl, cw=cw: g.tensor_copy(
                            ebuf[:, NW * sl:NW * sl + cw],
                            hbuf[:, NW * sl:NW * sl + cw]), sem="hb")
                        e.wait("gpsimd", "hb", j + 1)
                        e.dma("gpsimd", ssem,
                              w_out[128 * r:128 * r + 128, c0:c0 + cw],
                              ebuf[:, NW * sl:NW * sl + cw])
            cast(wqkvT_d, wqkv_bsc, D, QKVF)
            cast(woT_d, woT_bsc, QF, D)
            e.barrier()

        def ln_quant_sweep(e: E, a, AT, src_tile_fn, want_quant):
            LVT = (10 + 4 * NBK) if want_quant else 3
            XCOFF = 3
            for t in range(AT):
                tg = a * AT + t
                sl = t % 2
                e.wait("sync", "lv", (t - 2) * LVT + XCOFF)
                lsem = "ls" if sl == 0 else "ls2"
                e.dma("sync", lsem, xbuf[:, D * sl:D * sl + D], src_tile_fn(tg))
                e.wait("vector", lsem, 16 * (t // 2 + 1))
                nch = (D + 511) // 512
                for q in range(nch):
                    cw = min(512, D - 512 * q)
                    e.op("vector", lambda g, sl=sl, q=q, cw=cw: g.bn_stats(
                        bn6[:, 6 * q:6 * q + 6],
                        xbuf[:, D * sl + 512 * q:D * sl + 512 * q + cw]),
                        sem=("lv" if q == nch - 1 else None))
                e.wait("vector", "lv")
                e.op("vector", lambda g, nch=nch: g.bn_aggr(
                    mv[:, 0:2], bn6[:, 0:6 * nch]), sem="lv")
                e.wait("scalar", "lv", t * LVT + 2)
                e.op("scalar", lambda g: g.activation(
                    tmpc[:, 0:1], mv[:, 1:2], AF.Ln, bias=epsc[:, 0:1]), sem="lc")
                e.wait("scalar", "lc")
                e.op("scalar", lambda g: g.activation(
                    rrc[:, 0:1], tmpc[:, 0:1], AF.Exp, scale=-0.5),
                    sem=(None if want_quant else "lc"))
                if want_quant:
                    e.op("scalar", lambda g: g.activation(
                        rrc[:, 1:2], tmpc[:, 0:1], AF.Exp, scale=0.5), sem="lc")
                e.wait("vector", "lc", 2 * t + 2)
                e.wait("vector", "lt", t * NBK)
                if want_quant:
                    e.op("vector", lambda g, sl=sl: g.tensor_scalar(
                        hbuf[:, 0:D], xbuf[:, D * sl:D * sl + D], mv[:, 0:1],
                        None, ALU.subtract), sem="lv")
                    e.wait("vector", "lv")
                    e.op("vector", lambda g: g.tensor_reduce(
                        tmpc[:, 1:2], hbuf[:, 0:D], AX.X, ALU.max,
                        apply_absolute_value=True), sem="lv")
                    e.wait("vector", "lv")
                    e.op("vector", lambda g: g.tensor_scalar_mul(
                        tmpc[:, 2:3], rrc[:, 1:2], EPS), sem="lv")
                    e.wait("vector", "lv")
                    e.op("vector", lambda g: g.tensor_tensor(
                        tmpc[:, 3:4], tmpc[:, 1:2], tmpc[:, 2:3], ALU.max),
                        sem="lv")
                    e.wait("vector", "lv")
                    e.op("vector", lambda g: g.reciprocal(tmpc[:, 4:5], tmpc[:, 3:4]))
                    e.op("vector", lambda g: g.tensor_mul(
                        tmpc[:, 5:6], tmpc[:, 1:2], rrc[:, 0:1]), sem="lv")
                    e.wait("vector", "lv")
                    e.op("vector", lambda g: g.tensor_scalar_mul(
                        tmpc[:, 4:5], tmpc[:, 4:5], 127.0))
                    e.op("vector", lambda g, t=t: g.tensor_scalar(
                        sinv_a[:, t:t + 1], tmpc[:, 5:6], EPS, 1.0 / 127.0,
                        ALU.max, ALU.mult), sem="lv")
                    e.wait("vector", "lv")
                    e.op("vector", lambda g: g.tensor_scalar(
                        hbuf[:, 0:D], hbuf[:, 0:D], tmpc[:, 4:5], MAGIC,
                        ALU.mult, ALU.add), sem="lv")
                    e.wait("vector", "lv")
                    e.op("vector", lambda g: g.tensor_single_scalar(
                        hbuf[:, 0:D], hbuf[:, 0:D], MAGIC, ALU.subtract),
                        sem="lv")
                else:
                    e.op("vector", lambda g, sl=sl: g.tensor_scalar(
                        ebuf[:, 0:D], xbuf[:, D * sl:D * sl + D], mv[:, 0:1],
                        rrc[:, 0:1], ALU.subtract, ALU.mult), sem="lv")
                e.wait("tensor", "lv", t * LVT + (10 if want_quant else 3))
                xv = big_a[:, 0:DT * ASC].rearrange("p (d t) -> p d t", d=DT)
                for bk in range(NBK):
                    nblk = min(4, DT - 4 * bk)
                    gi = t * NBK + bk
                    e.wait("tensor", "lq", gi - 1)
                    for bi in range(nblk):
                        d = 4 * bk + bi
                        src = hbuf if want_quant else ebuf
                        idd = identf if want_quant else identb
                        if want_quant:
                            e.op("tensor", lambda g, d=d, bk=bk, bi=bi, src=src,
                                 idd=idd:
                                 g.transpose(ps[bk % 2][:, 128 * bi:128 * bi + 128],
                                             src[:, 128 * d:128 * d + 128], idd[:]),
                                 sem=("lt" if bi == nblk - 1 else None))
                        else:
                            e.op("tensor", lambda g, d=d, bk=bk, bi=bi, src=src,
                                 idd=idd:
                                 g.transpose(psbf(bk % 2)[:, 128 * bi:128 * bi + 128],
                                             src[:, 128 * d:128 * d + 128], idd[:]),
                                 sem=("lt" if bi == nblk - 1 else None))
                    if want_quant:
                        e.wait("vector", "lt", gi + 1)
                        e.wait("vector", "lq", gi)
                        e.op("vector", lambda g, bk=bk, nblk=nblk: g.tensor_copy(
                            i32b[:, 0:128 * nblk], ps[bk % 2][:, 0:128 * nblk]),
                            sem="lv")
                        e.wait("vector", "lv")
                        e.op("vector", lambda g, bk=bk, nblk=nblk: g.tensor_sub(
                            obuf[:, 0:128 * nblk], ps[bk % 2][:, 0:128 * nblk],
                            i32b[:, 0:128 * nblk]), sem="lv")
                        e.wait("vector", "lv")
                        e.op("vector", lambda g, nblk=nblk: g.tensor_scalar_mul(
                            obuf[:, 0:128 * nblk], obuf[:, 0:128 * nblk], 2.0),
                            sem="lv")
                        e.wait("vector", "lv")
                        e.op("vector", lambda g, nblk=nblk: g.tensor_copy(
                            i32c[:, 0:128 * nblk], obuf[:, 0:128 * nblk]),
                            sem="lv")
                        e.wait("vector", "lv")
                        e.wait("vector", "lq", gi)
                        e.op("vector", lambda g, bk=bk, nblk=nblk, t=t:
                             g.tensor_add(
                                 xv[:, 4 * bk:4 * bk + nblk,
                                    128 * t:128 * t + 128],
                                 i32b[:, 0:128 * nblk].rearrange(
                                     "p (d t) -> p d t", d=nblk),
                                 i32c[:, 0:128 * nblk].rearrange(
                                     "p (d t) -> p d t", d=nblk)), sem="lq")
                    else:
                        e.wait("scalar", "lt", gi + 1)
                        e.op("scalar", lambda g, bk=bk, nblk=nblk, t=t: g.activation(
                            xv[:, 4 * bk:4 * bk + nblk, 128 * t:128 * t + 128],
                            psbf(bk % 2)[:, 0:128 * nblk].rearrange(
                                "p (d t) -> p d t", d=nblk),
                            AF.Copy), sem="lq")

        def ffn_phase(e: E, Ttok, in_tile_fn, resid_tile_fn, out_tile_fn):
            nA = Ttok // ASC
            AT = ASC // 128
            BT = BSC // 128
            xqv = big_a[:, 0:DT * ASC].rearrange("p (d t) -> p d t", d=DT)
            hqv = big_b[:, 0:FT * BSC].rearrange("p (f t) -> p f t", f=FT)
            for a in range(nA):
                ln_quant_sweep(e, a, AT, in_tile_fn, want_quant=True)
                e.barrier()
                if dbg_xq is not None:
                    slot = dbg_slot[0]
                    dbg_slot[0] += 1
                    e.dma("gpsimd", "qd", dbg_xq[128 * slot:128 * slot + 128, :],
                          big_a[:, 0:DT * ASC])
                    e.dma("gpsimd", "qd",
                          dbg_st[128 * slot:128 * slot + 128, :],
                          sinv_a[:, 0:AT])
                    e.barrier()
                e.op("vector", lambda g: g.memset(hmax_a[:, 0:AT], 0.0), sem="lv")
                for ci, (c0, cw) in enumerate(FCH):
                    DQ = max(1, DT // 4)
                    e.wait("sync", "gb", 2 * ci * AT - 1)
                    for d1 in range(0, DT, DQ):
                        e.dma("sync", "wga",
                              wbuf[:, NW * d1:NW * (d1 + DQ)].rearrange(
                                  "p (d c) -> p d c", d=DQ)[:, :, 0:cw],
                              wgq_sc[128 * d1:128 * (d1 + DQ), c0:c0 + cw]
                              .rearrange("(d p) c -> p d c", d=DQ))
                    e.wait("sync", "gb", 2 * ci * AT)
                    for d1 in range(0, DT, DQ):
                        e.dma("sync", "wgb",
                              wbuf[:, DT * NW + NW * d1:
                                   DT * NW + NW * (d1 + DQ)].rearrange(
                                  "p (d c) -> p d c", d=DQ)[:, :, 0:cw],
                              wuq_sc[128 * d1:128 * (d1 + DQ), c0:c0 + cw]
                              .rearrange("(d p) c -> p d c", d=DQ))
                    e.wait("tensor", "wga", 64 * (ci + 1))
                    e.wait("tensor", "wgb", 64 * (ci + 1))
                    for t in range(AT):
                        n = ci * AT + t
                        pbg = 2 + (t % 2)
                        e.wait("tensor", "gc", n - 1)
                        for dti in range(DT):
                            e.op("tensor",
                                 lambda g, pbg=pbg, dti=dti, t=t, cw=cw:
                                 g.matmul(ps[pbg][:, 0:cw],
                                          xqv[:, dti, 128 * t:128 * t + 128],
                                          wbuf[:, NW * dti:NW * dti + cw],
                                          start=(dti == 0),
                                          stop=(dti == DT - 1)),
                                 sem=("gb" if dti == DT - 1 else None))
                        pbu = 4 + (t % 2)
                        e.wait("tensor", "gd", 3 * n - 5)
                        for dti in range(DT):
                            e.op("tensor",
                                 lambda g, pbu=pbu, dti=dti, t=t, cw=cw:
                                 g.matmul(ps[pbu][:, 0:cw],
                                          xqv[:, dti, 128 * t:128 * t + 128],
                                          wbuf[:, DT * NW + NW * dti:
                                               DT * NW + NW * dti + cw],
                                          start=(dti == 0),
                                          stop=(dti == DT - 1)),
                                 sem=("gb" if dti == DT - 1 else None))
                        e.wait("scalar", "gb", 2 * n + 1)
                        e.wait("scalar", "gd", 3 * n - 5)
                        e.op("scalar", lambda g, pbg=pbg, t=t, cw=cw:
                             g.activation(
                                 hbuf[:, NW * (t % 2):NW * (t % 2) + cw],
                                 ps[pbg][:, 0:cw], AF.Silu,
                                 scale=sinv_a[:, t:t + 1]), sem="gc")
                        e.wait("vector", "gb", 2 * n + 2)
                        e.wait("vector", "gc", n + 1)
                        e.wait("vector", "ge" if n % 2 == 0 else "gy",
                               16 * (n // 2))
                        e.op("vector", lambda g, pbu=pbu, t=t, cw=cw:
                             g.scalar_tensor_tensor(
                                 hbuf[:, NW * (2 + t % 2):NW * (2 + t % 2) + cw],
                                 ps[pbu][:, 0:cw], sinv_a[:, t:t + 1],
                                 hbuf[:, NW * (t % 2):NW * (t % 2) + cw],
                                 ALU.mult, ALU.mult), sem="gd")
                        e.wait("vector", "gd")
                        e.op("vector", lambda g, t=t, cw=cw: g.tensor_reduce(
                            tmpc[:, 6:7],
                            hbuf[:, NW * (2 + t % 2):NW * (2 + t % 2) + cw],
                            AX.X, ALU.max, apply_absolute_value=True), sem="gd")
                        e.wait("vector", "gd")
                        e.op("vector", lambda g, t=t: g.tensor_tensor(
                            hmax_a[:, t:t + 1], hmax_a[:, t:t + 1],
                            tmpc[:, 6:7], ALU.max), sem="gd")
                        e.wait("gpsimd", "gd", 3 * n + 1)
                        e.dma("gpsimd", "ge" if n % 2 == 0 else "gy",
                              h_sc[a * ASC + 128 * t:a * ASC + 128 * t + 128,
                                   c0:c0 + cw],
                              hbuf[:, NW * (2 + t % 2):NW * (2 + t % 2) + cw])
                e.barrier()
                e.op("vector", lambda g: g.tensor_scalar_max(
                    ch_a[:, 0:AT], hmax_a[:, 0:AT], EPS), sem="lv")
                e.wait("vector", "lv")
                e.op("vector", lambda g: g.tensor_scalar_mul(
                    hs_a[:, 0:AT], ch_a[:, 0:AT], 0.5 / 127.0), sem="lv")
                e.wait("vector", "lv")
                e.op("vector", lambda g: g.reciprocal(ch_a[:, 0:AT], ch_a[:, 0:AT]),
                     sem="lv")
                e.wait("vector", "lv")
                e.op("vector", lambda g: g.tensor_scalar_mul(
                    ch_a[:, 0:AT], ch_a[:, 0:AT], 127.0), sem="lv")
                e.barrier()
                NQ = BT * NFC
                ND = len(DCH) * NP
                hqs = [big_b[:, 0:FT * BSC].rearrange(
                           "p (f t) -> p f t", f=FT),
                       big_a[:, 0:FT * BSC].rearrange(
                           "p (f t) -> p f t", f=FT)]
                NSC = ASC // BSC

                def quant_group(k, tb, ci):
                    b0 = a * ASC + k * BSC
                    tA0 = k * BT
                    hql = hqs[k % 2]
                    c0, cw = FCH[ci]
                    q = k * NQ + tb * NFC + ci
                    sl = q % 4
                    e.wait("sync", "hc", q - 3)
                    lsem = ["ha", "ha2", "qa", "qa2"][q % 4]
                    e.dma("sync", lsem, hbuf[:, NW * sl:NW * sl + cw],
                          h_sc[b0 + 128 * tb:b0 + 128 * tb + 128,
                               c0:c0 + cw])
                    e.wait("scalar", lsem, 16 * (q // 4 + 1))
                    e.op("scalar", lambda g, sl=sl, cw=cw, tl=tA0 + tb:
                         g.activation(
                             hbuf[:, NW * sl:NW * sl + cw],
                             hbuf[:, NW * sl:NW * sl + cw], AF.Relu,
                             scale=ch_a[:, tl:tl + 1],
                             bias=epsc[:, 2:3]), sem="hb")
                    e.wait("tensor", "hb", q + 1)
                    e.wait("tensor", "hd", 4 * (q - 4) + 2)
                    nblk = (cw + 127) // 128
                    pb = [0, 1, 6, 7][q % 4]
                    for b2 in range(nblk):
                        e.op("tensor", lambda g, sl=sl, b2=b2, pb=pb:
                             g.transpose(
                                 ps[pb][:, 128 * b2:128 * b2 + 128],
                                 hbuf[:, NW * sl + 128 * b2:
                                      NW * sl + 128 * b2 + 128],
                                 identf[:]),
                             sem=("hc" if b2 == nblk - 1 else None))
                    e.wait("vector", "hc", q + 1)
                    e.wait("vector", "hd", 4 * q)
                    e.op("vector", lambda g, pb=pb, nblk=nblk:
                         g.tensor_single_scalar(
                             i32b[:, 0:128 * nblk], ps[pb][:, 0:128 * nblk],
                             MAGIC, ALU.subtract), sem="hd")
                    e.wait("vector", "hd")
                    e.op("vector", lambda g, pb=pb, nblk=nblk:
                         g.scalar_tensor_tensor(
                             obuf[:, 0:128 * nblk], ps[pb][:, 0:128 * nblk],
                             -MAGIC, i32b[:, 0:128 * nblk],
                             ALU.add, ALU.subtract), sem="hd")
                    e.wait("vector", "hd")
                    e.op("vector", lambda g, nblk=nblk:
                         g.tensor_single_scalar(
                             i32c[:, 0:128 * nblk], obuf[:, 0:128 * nblk],
                             2.0, ALU.mult), sem="hd")
                    e.wait("vector", "hd")
                    e.op("vector", lambda g, ci=ci, nblk=nblk, tb=tb, hql=hql:
                         g.tensor_add(
                             hql[:, (NW // 128) * ci:
                                 (NW // 128) * ci + nblk,
                                 128 * tb:128 * tb + 128],
                             i32b[:, 0:128 * nblk].rearrange(
                                 "p (f t) -> p f t", f=nblk),
                             i32c[:, 0:128 * nblk].rearrange(
                                 "p (f t) -> p f t", f=nblk)), sem="hd")

                def down_part(j, di, pj):
                    b0 = a * ASC + j * BSC
                    tA0 = j * BT
                    hql = hqs[j % 2]
                    d0, dw = DCH[di]
                    f0, fn = W3[pj]
                    part = (j * len(DCH) + di) * NP + pj
                    wsl3 = part % 2
                    e.wait("sync", "hf", part - 1)
                    lsem = "he" if wsl3 == 0 else "he2"
                    for f1 in range(0, fn, 4):
                        f2 = min(fn, f1 + 4)
                        e.dma("sync", lsem,
                              wbuf[:, G3 * NW * wsl3 + NW * f1:
                                   G3 * NW * wsl3 + NW * f2]
                              .rearrange("p (f c) -> p f c", f=f2 - f1)
                              [:, :, 0:dw],
                              wdq_sc[128 * (f0 + f1):128 * (f0 + f2),
                                     d0:d0 + dw]
                              .rearrange("(f p) c -> p f c", f=f2 - f1))
                    e.wait("tensor", lsem)
                    if di == 0 and pj == 0:
                        e.wait("tensor", "hd", 4 * (j + 1) * NQ)
                    for tb in range(BT):
                        pb = 2 + tb
                        if pj == 0:
                            e.wait("tensor", "hg",
                                   (j * len(DCH) + di - 1) * BT + tb + 1)
                        for fi in range(fn):
                            e.op("tensor",
                                 lambda g, pb=pb, f0=f0, fi=fi, tb=tb,
                                 wsl3=wsl3, dw=dw, hql=hql:
                                 g.matmul(
                                     ps[pb][:, 0:dw],
                                     hql[:, f0 + fi,
                                         128 * tb:128 * tb + 128],
                                     wbuf[:, G3 * NW * wsl3 + NW * fi:
                                          G3 * NW * wsl3 + NW * fi + dw],
                                     start=(f0 + fi == 0),
                                     stop=(f0 + fi == FT - 1)),
                                 sem=("hf" if (fi == fn - 1 and
                                               tb == BT - 1) else None))
                    if pj != NP - 1:
                        return
                    for tb in range(BT):
                        pb = 2 + tb
                        tg = (b0 + 128 * tb) // 128
                        idx = (j * len(DCH) + di) * BT + tb
                        sl = idx % 2
                        ssem = "ge" if sl == 0 else "gy"
                        e.wait("sync", "hg", idx - 1)
                        lsem = "ls" if sl == 0 else "ls2"
                        e.dma("sync", lsem,
                              xbuf[:, D * sl + d0:D * sl + d0 + dw],
                              resid_tile_fn(tg)[:, d0:d0 + dw])
                        e.wait("vector", lsem, 16 * (idx // 2 + 1))
                        e.wait("vector", "hf", (j * len(DCH) + di + 1) * NP)
                        e.wait("vector", ssem, 16 * (idx // 2))
                        e.op("vector", lambda g, pb=pb, sl=sl, d0=d0, dw=dw,
                             tl=tA0 + tb:
                             g.scalar_tensor_tensor(
                                 obuf[:, 512 + NW * sl:512 + NW * sl + dw],
                                 ps[pb][:, 0:dw], hs_a[:, tl:tl + 1],
                                 xbuf[:, D * sl + d0:D * sl + d0 + dw],
                                 ALU.mult, ALU.add), sem="hg")
                        e.wait("gpsimd", "hg", idx + 1)
                        e.dma("gpsimd", ssem,
                              out_tile_fn(tg)[:, d0:d0 + dw],
                              obuf[:, 512 + NW * sl:512 + NW * sl + dw])

                for k in range(NSC):
                    qgs = [(k, tb, ci) for tb in range(BT)
                           for ci in range(NFC)]
                    dps = ([(k - 1, di, pj) for di in range(len(DCH))
                            for pj in range(NP)] if k >= 1 else [])
                    e.wait("vector", "hf", ND * (k - 1))
                    di_ = 0
                    for i, (kk, tb, ci) in enumerate(qgs):
                        quant_group(kk, tb, ci)
                        while dps and di_ * len(qgs) <= (i + 1) * len(dps):
                            down_part(*dps.pop(0))
                            di_ += 1
                    for dp in dps:
                        down_part(*dp)
                for dp in [(NSC - 1, di, pj) for di in range(len(DCH))
                           for pj in range(NP)]:
                    down_part(*dp)
                e.barrier()

        def p3_qkv(e: E):
            AT = ASC // 128
            xlv = big_a[:, 0:DT * ASC].rearrange("p (d t) -> p d t", d=DT)
            e.op("vector", lambda g: g.memset(srow[0:1, 16:112], 0.0), sem="af")
            for a in range(TA // ASC):
                is_own = (a * ASC) >= TC
                ln_quant_sweep(e, a, AT,
                               lambda tg: x2_sc[128 * tg:128 * tg + 128, :],
                               want_quant=False)
                e.barrier()
                fstart = 0 if is_own else QF
                fchunks = [("qk", f, min(NW, QF + KF - f))
                           for f in range(fstart, QF + KF, NW)]
                vchunks = [("v", QF + KF + i * NW, min(NW, VF - i * NW))
                           for i in range((VF + NW - 1) // NW)]
                groups_per = {}
                qb_hist = []
                for kind, f0, fw in fchunks + vchunks:
                    ngr = (fw // 128) * (ASC // NW) if kind == "qk" else AT
                    qb_hist.append(ngr)
                norm_pend = []

                def flush_norm(e=e):
                    if not norm_pend:
                        return
                    m, col = norm_pend.pop()
                    e.wait("tensor", "ab", m + 1)
                    e.wait("tensor", "af", 2 * (m - 1))
                    e.op("tensor", lambda g, m=m: g.matmul(
                        ps[4 + m % 2][0:1, 0:NW], onescf[:],
                        hbuf[:, NW * (m % 2):NW * (m % 2) + NW],
                        start=True, stop=True), sem="ac")
                    e.wait("vector", "ac", m + 1)
                    e.wait("vector", "af", 2 * m)
                    e.op("vector", lambda g, m=m: g.reduce_max(
                        tmpc[0:1, 0:1], ps[4 + m % 2][0:1, 0:NW], AX.X),
                        sem="af")
                    e.wait("vector", "af")
                    e.op("vector", lambda g, col=col: g.tensor_tensor(
                        srow[0:1, col:col + 1], srow[0:1, col:col + 1],
                        tmpc[0:1, 0:1], ALU.max), sem="af")
                for cj, (kind, f0, fw) in enumerate(fchunks + vchunks):
                    wsl = cj % 2
                    e.wait("sync", "qb",
                           sum(qb_hist[:cj - 1]) if cj >= 2 else 0)
                    lsem = "qa" if wsl == 0 else "qa2"
                    DQ3 = max(1, DT // 4)
                    for d1 in range(0, DT, DQ3):
                        e.dma("sync", lsem,
                              wbuf[:, DT * NW * wsl + NW * d1:
                                   DT * NW * wsl + NW * (d1 + DQ3)]
                              .rearrange("p (d c) -> p d c", d=DQ3)
                              [:, :, 0:fw],
                              wqkv_bsc[128 * d1:128 * (d1 + DQ3), f0:f0 + fw]
                              .rearrange("(d p) c -> p d c", d=DQ3))
                    e.wait("tensor", lsem)
                    if kind == "qk":
                        it = [(fi, tc) for fi in range(fw // 128)
                              for tc in range(ASC // NW)]
                    else:
                        it = [(0, t) for t in range(AT)]
                    for (fi, tc) in it:
                        gidx = e.loc("qb")
                        pb = 2 + gidx % 2
                        e.wait("tensor", "qc", gidx - 1)
                        if kind == "qk":
                            for dti in range(DT):
                                e.op("tensor",
                                     lambda g, pb=pb, wsl=wsl, fi=fi, dti=dti, tc=tc:
                                     g.matmul(
                                         ps[pb][:, 0:NW],
                                         wbuf[:, DT * NW * wsl + NW * dti + 128 * fi:
                                              DT * NW * wsl + NW * dti + 128 * fi + 128],
                                         xlv[:, dti, NW * tc:NW * tc + NW],
                                         start=(dti == 0), stop=False))
                            e.op("tensor", lambda g, pb=pb, f0=f0, fi=fi:
                                 g.matmul(ps[pb][:, 0:NW],
                                          bqkvrow[0:1, f0 + 128 * fi:
                                                  f0 + 128 * fi + 128],
                                          onesrow[0:1, 0:NW],
                                          start=False, stop=True), sem="qb")
                        else:
                            t = tc
                            for dti in range(DT):
                                e.op("tensor",
                                     lambda g, pb=pb, wsl=wsl, dti=dti, t=t, fw=fw:
                                     g.matmul(
                                         ps[pb][:, 0:fw],
                                         xlv[:, dti, 128 * t:128 * t + 128],
                                         wbuf[:, DT * NW * wsl + NW * dti:
                                              DT * NW * wsl + NW * dti + fw],
                                         start=(dti == 0), stop=False))
                            e.op("tensor", lambda g, pb=pb, f0=f0, fw=fw:
                                 g.matmul(ps[pb][:, 0:fw], ones1[:],
                                          bqkvrow[0:1, f0:f0 + fw],
                                          start=False, stop=True), sem="qb")
                        flush_norm()
                        e.wait("scalar", "qb")
                        e.wait("scalar", "qd" if gidx % 2 == 0 else "qe",
                               16 * (gidx // 2))
                        esl = gidx % 2
                        ew = NW if kind == "qk" else fw
                        e.op("scalar", lambda g, pb=pb, esl=esl, ew=ew: g.activation(
                            ebuf[:, NW * esl:NW * esl + ew], ps[pb][:, 0:ew],
                            AF.Copy), sem="qc")
                        if kind == "qk":
                            m = e.loc("ab")
                            f = f0 + 128 * fi
                            col = (16 + f // 128) if f < QF else (32 + (f - QF) // 128)
                            e.wait("scalar", "ac", m - 1)
                            e.op("scalar", lambda g, m=m, esl=esl: g.activation(
                                hbuf[:, NW * (m % 2):NW * (m % 2) + NW],
                                ebuf[:, NW * esl:NW * esl + NW],
                                AF.Square), sem="ab")
                            norm_pend.append((m, col))
                        e.wait("gpsimd", "qc", gidx + 1)
                        qsem = "qd" if gidx % 2 == 0 else "qe"
                        if kind == "qk":
                            e.dma("gpsimd", qsem,
                                  qk_sc[f0 + 128 * fi:f0 + 128 * fi + 128,
                                        a * ASC + NW * tc:a * ASC + NW * tc + NW],
                                  ebuf[:, NW * esl:NW * esl + NW])
                        else:
                            e.dma("gpsimd", qsem,
                                  v_sc[a * ASC + 128 * tc:a * ASC + 128 * tc + 128,
                                       f0 - QF - KF:f0 - QF - KF + fw],
                                  ebuf[:, NW * esl:NW * esl + fw])
                flush_norm()
                e.barrier()

        def p4_attn(e: E):
            QCN = TO // QC
            ofv = big_a[:, 0:2 * PH * TO].rearrange("p (f t) -> p f t", f=2 * PH)
            kb0, qb0, vb0 = 0, 2 * TA, 2 * TA + 2 * TO
            vext_v = big_b[:, vb0:vb0 + KT * VEXT].rearrange(
                "p (k c) -> p k c", k=KT)
            e.op("vector", lambda g: g.memset(vext_v[:, :, 256:257], 1.0),
                 sem="af")
            for kv in range(KVPH):
                for r in range(2):
                    e.op("vector", lambda g, kv=kv, r=r: g.tensor_copy(
                        srow[0:1, 48 + 4 * kv + 2 * r:50 + 4 * kv + 2 * r],
                        srow[0:1, 32 + 2 * kv:34 + 2 * kv]),
                        sem=("af" if (kv == KVPH - 1 and r == 1) else None))
            e.wait("vector", "af")
            e.op("vector", lambda g: g.tensor_mul(
                srow[0:1, 64:64 + 2 * PH], srow[0:1, 16:16 + 2 * PH],
                srow[0:1, 48:48 + 2 * PH]), sem="af")
            e.wait("scalar", "af")
            e.op("scalar", lambda g: g.activation(
                srow[0:1, 64:64 + 2 * PH], srow[0:1, 64:64 + 2 * PH], AF.Ln,
                bias=epsc[0:1, 1:2]), sem="oa")
            e.wait("scalar", "oa")
            e.op("scalar", lambda g: g.activation(
                srow[0:1, 64:64 + 2 * PH], srow[0:1, 64:64 + 2 * PH], AF.Exp,
                scale=0.5), sem="oa")
            e.wait("vector", "oa")
            e.op("vector", lambda g: g.tensor_scalar_mul(
                srow[0:1, 64:64 + 2 * PH], srow[0:1, 64:64 + 2 * PH],
                -ALPHA), sem="af")
            e.wait("vector", "af")
            e.op("vector", lambda g: g.tensor_scalar(
                srow[0:1, 80:80 + 2 * PH], srow[0:1, 64:64 + 2 * PH],
                srow[0:1, 8:9], None, ALU.add), sem="af")
            e.wait("vector", "af")
            e.op("vector", lambda g: g.tensor_copy(
                ebuf[0:1, 32:64], srow[0:1, 64:96]), sem="af")
            e.wait("tensor", "af")
            e.op("tensor", lambda g: g.matmul(
                ps[6][:, 0:32], ones1[:], ebuf[0:1, 32:64],
                start=True, stop=True), sem="ac")
            e.wait("scalar", "ac")
            e.op("scalar", lambda g: g.activation(
                mtab[:, 0:32], ps[6][:, 0:32], AF.Copy), sem="oa")
            e.barrier()
            SBKS = [0, 1, 6]
            for ph in range(PH):
                kvph = ph // 2
                if ph % 2 == 0:
                    for c in range(2):
                        e.dma("sync", "aa",
                              big_b[:, kb0 + TA * c:kb0 + TA * c + TA],
                              qk_sc[QF + kvph * 256 + 128 * c:
                                    QF + kvph * 256 + 128 * c + 128, :])
                    KQ = max(1, KT // 4)
                    for k1 in range(0, KT, KQ):
                        e.dma("sync", "aa", vext_v[:, k1:k1 + KQ, 0:256],
                              v_sc[128 * k1:128 * (k1 + KQ),
                                   kvph * 256:kvph * 256 + 256]
                              .rearrange("(k p) c -> p k c", k=KQ))
                for c in range(2):
                    e.dma("sync", "aa",
                          big_b[:, qb0 + TO * c:qb0 + TO * c + TO],
                          qk_sc[ph * 256 + 128 * c:ph * 256 + 128 * c + 128, TC:])
                e.wait("tensor", "aa")
                for qc in range(QCN):
                    allb = []
                    for comp in range(2):
                        ctx_kts = list(range(TC // 128))
                        own_full = [TC // 128 + j for j in range(qc * QS)]
                        band_kts = [TC // 128 + qc * QS + j for j in range(QS)]
                        bs = []
                        for i in range(0, len(ctx_kts), 2):
                            bs.append((comp, ctx_kts[i:i + 2], True, ()))
                        for i in range(0, len(own_full), 2):
                            bs.append((comp, own_full[i:i + 2], False, ()))
                        for kt in band_kts:
                            bs.append((comp, [kt], False, (kt,)))
                        allb += [bi + (j == 0, j == len(bs) - 1)
                                 for j, bi in enumerate(bs)]
                    NB = len(allb)
                    ads = []
                    afb = {}

                    def emit_scores(n, e=e, allb=allb, ads=ads):
                        comp, kts_b, is_ctx, bands, first, last = allb[n]
                        g0 = e.loc("ad")
                        sbk = SBKS[g0 % 3]
                        e.wait("tensor", "ae", 2 * (g0 - 2))
                        nb = len(kts_b)
                        for bi_k, kt in enumerate(kts_b):
                            e.op("tensor",
                                 lambda g, sbk=sbk, comp=comp, kt=kt, qc=qc,
                                 bi_k=bi_k:
                                 g.matmul(
                                     ps[sbk][:, QC * bi_k:QC * bi_k + QC],
                                     big_b[:, kb0 + TA * comp + 128 * kt:
                                          kb0 + TA * comp + 128 * kt + 128],
                                     big_b[:, qb0 + TO * comp + QC * qc:
                                          qb0 + TO * comp + QC * qc + QC],
                                     start=True, stop=True),
                                 sem=("ad" if bi_k == nb - 1 else None))
                        ads.append(e.loc("ad"))
                    emit_scores(0)
                    for n in range(NB):
                        comp, kts_b, is_ctx, bands, first, last = allb[n]
                        if n + 1 < NB:
                            emit_scores(n + 1)
                        nb = len(kts_b)
                        sbk = SBKS[(ads[n] - 1) % 3]
                        for kt in bands:
                            dj = kt - TC // 128 - qc * QS
                            e.wait("vector", "ad", ads[n])
                            e.op("vector", lambda g, sbk=sbk, dj=dj:
                                 g.tensor_add(ps[sbk][:, 0:QC],
                                              ps[sbk][:, 0:QC],
                                              bandm[:, QC * dj:QC * dj + QC]),
                                 sem="af")
                            afb[n] = e.loc("af")
                        if n in afb:
                            e.wait("scalar", "af", afb[n])
                        else:
                            e.wait("scalar", "ad", ads[n])
                        bcol = (2 * ph + comp) + (16 if is_ctx else 0)
                        i_sc = e.loc("ae") // 2
                        esl = i_sc % 4
                        e.wait("scalar", "aq", i_sc - 3)
                        for qs in range(QS):
                            e.op("scalar", lambda g, sbk=sbk, esl=esl,
                                 bcol=bcol, nb=nb, qs=qs:
                                 g.activation(
                                     ebuf[:, 512 * esl:512 * esl + QC * nb]
                                     .rearrange("p (b q) -> p b q", b=nb)
                                     [:, :, 128 * qs:128 * qs + 128],
                                     ps[sbk][:, 0:QC * nb]
                                     .rearrange("p (b q) -> p b q", b=nb)
                                     [:, :, 128 * qs:128 * qs + 128],
                                     AF.Exp, scale=ALPHA,
                                     bias=mtab[:, bcol:bcol + 1]), sem="ae")
                            e.wait("tensor", "ae")
                            pb = 2 + 2 * comp + qs
                            if first:
                                e.wait("tensor", "ao",
                                       (qc - 1) * QS + qs + 1)
                            for bi_k, kt in enumerate(kts_b):
                                e.op("tensor",
                                     lambda g, pb=pb, esl=esl, qs=qs, kt=kt,
                                     bi_k=bi_k, first=first, last=last, nb=nb:
                                     g.matmul(
                                         ps[pb][:, 0:VEXT],
                                         ebuf[:, 512 * esl + QC * bi_k + 128 * qs:
                                              512 * esl + QC * bi_k + 128 * qs + 128],
                                         vext_v[:, kt, :],
                                         start=(first and bi_k == 0),
                                         stop=(last and bi_k == nb - 1)),
                                     sem=("aq" if (qs == QS - 1 and
                                                   bi_k == nb - 1) else None))
                    e.wait("vector", "aq", e.loc("ae") // 2)
                    for qs in range(QS):
                        pb1, pb2 = 2 + qs, 4 + qs
                        osl = 512 * (qc % 2) + 256 * qs
                        sqs = 1024 + 256 * qs
                        e.op("vector", lambda g, pb1=pb1: g.reciprocal(
                            tmpc[:, 0:1], ps[pb1][:, 256:257]))
                        e.op("vector", lambda g, pb2=pb2: g.reciprocal(
                            tmpc[:, 1:2], ps[pb2][:, 256:257]), sem="af")
                        e.wait("vector", "af")
                        e.op("vector", lambda g: g.tensor_mul(
                            tmpc[:, 2:3], tmpc[:, 1:2], lamncol[:]), sem="af")
                        e.wait("vector", "af")
                        e.wait("vector", "ab", 4 * (qc - 1))
                        e.op("vector", lambda g, pb1=pb1, osl=osl: g.tensor_scalar(
                            obuf[:, osl:osl + 256], ps[pb1][:, 0:256],
                            tmpc[:, 0:1], None, ALU.mult), sem="af")
                        e.wait("vector", "af")
                        e.op("vector", lambda g, pb2=pb2, osl=osl:
                             g.scalar_tensor_tensor(
                                 obuf[:, osl:osl + 256], ps[pb2][:, 0:256],
                                 tmpc[:, 2:3], obuf[:, osl:osl + 256],
                                 ALU.mult, ALU.add), sem="ao")
                        e.wait("scalar", "ao")
                        e.op("scalar", lambda g, osl=osl, sqs=sqs: g.activation(
                            obuf[:, sqs:sqs + 256], obuf[:, osl:osl + 256],
                            AF.Square), sem="oa")
                        e.wait("vector", "oa")
                        e.op("vector", lambda g, sqs=sqs: g.reduce_sum(
                            tmpc[:, 3:4], obuf[:, sqs:sqs + 256], AX.X),
                            sem="af")
                        e.wait("scalar", "af")
                        e.op("scalar", lambda g: g.activation(
                            tmpc[:, 4:5], tmpc[:, 3:4], AF.Ln,
                            scale=1.0 / 256.0, bias=epsc[:, 0:1]), sem="oa")
                        e.wait("scalar", "oa")
                        e.op("scalar", lambda g: g.activation(
                            tmpc[:, 4:5], tmpc[:, 4:5], AF.Exp, scale=-0.5),
                            sem="oa")
                        e.wait("vector", "oa")
                        e.op("vector", lambda g, osl=osl: g.scalar_tensor_tensor(
                            obuf[:, osl:osl + 256], obuf[:, osl:osl + 256],
                            tmpc[:, 4:5], swtile[:], ALU.mult, ALU.mult),
                            sem="ac")

                    def emit_tevac(qcm, e=e, ph=ph):
                        e.wait("tensor", "ac", 2 * qcm + 2)
                        e.wait("tensor", "ag", 2 * qcm)
                        for qs in range(QS):
                            for eh in range(2):
                                e.op("tensor", lambda g, qcm=qcm, qs=qs, eh=eh:
                                     g.transpose(
                                         ps[7][:, 128 * (2 * eh + qs):
                                               128 * (2 * eh + qs) + 128],
                                         obuf[:, 512 * (qcm % 2) + 256 * qs +
                                              128 * eh:
                                              512 * (qcm % 2) + 256 * qs +
                                              128 * eh + 128],
                                         identf[:]), sem="ab")
                        e.wait("scalar", "ab")
                        for eh in range(2):
                            e.op("scalar", lambda g, eh=eh, qcm=qcm, ph=ph:
                                 g.activation(
                                     ofv[:, 2 * ph + eh,
                                         256 * qcm:256 * qcm + 256],
                                     ps[7][:, 256 * eh:256 * eh + 256],
                                     AF.Copy), sem="ag")
                    if qc > 0:
                        emit_tevac(qc - 1)
                emit_tevac(QCN - 1)
                e.barrier()

        def p5_outproj(e: E):
            ofv = big_a[:, 0:2 * PH * TO].rearrange("p (f t) -> p f t", f=2 * PH)
            OFT = 2 * PH
            NT = TO // 128
            for di, (d0, dw) in enumerate(DCH):
                wsl = di % 2
                e.wait("sync", "oa", (di - 1) * NT)
                lsem = "ob" if wsl == 0 else "ob2"
                for fi in range(OFT):
                    e.dma("sync", lsem,
                          wbuf[:, OFT * NW * wsl + NW * fi:
                               OFT * NW * wsl + NW * fi + dw],
                          woT_bsc[128 * fi:128 * fi + 128, d0:d0 + dw])
                e.wait("tensor", lsem)
                for t in range(NT):
                    gidx = di * NT + t
                    pb = gidx % 2
                    e.wait("tensor", "oc", gidx - 1)
                    for fi in range(OFT):
                        e.op("tensor", lambda g, pb=pb, wsl=wsl, fi=fi, t=t, dw=dw:
                             g.matmul(ps[pb][:, 0:dw],
                                      ofv[:, fi, 128 * t:128 * t + 128],
                                      wbuf[:, OFT * NW * wsl + NW * fi:
                                           OFT * NW * wsl + NW * fi + dw],
                                      start=(fi == 0), stop=False))
                    e.op("tensor", lambda g, pb=pb, d0=d0, dw=dw:
                         g.matmul(ps[pb][:, 0:dw], ones1[:],
                                  borow[0:1, d0:d0 + dw],
                                  start=False, stop=True), sem="oa")
                    sl = gidx % 2
                    e.wait("sync", "oc", gidx - 1)
                    lsem = "od" if sl == 0 else "od2"
                    e.dma("sync", lsem, xbuf[:, D * sl + d0:D * sl + d0 + dw],
                          x2_sc[TC + 128 * t:TC + 128 * t + 128, d0:d0 + dw])
                    e.wait("vector", lsem, 16 * (gidx // 2 + 1))
                    e.wait("vector", "oa", gidx + 1)
                    e.wait("vector", "qd" if gidx % 2 == 0 else "qe",
                           16 * (gidx // 2))
                    e.op("vector", lambda g, pb=pb, sl=sl, d0=d0, dw=dw:
                         g.tensor_add(hbuf[:, NW * sl:NW * sl + dw],
                                      ps[pb][:, 0:dw],
                                      xbuf[:, D * sl + d0:D * sl + d0 + dw]),
                         sem="oc")
                    e.wait("gpsimd", "oc", gidx + 1)
                    e.dma("gpsimd", "qd" if gidx % 2 == 0 else "qe",
                          x3_sc[128 * t:128 * t + 128, d0:d0 + dw],
                          hbuf[:, NW * sl:NW * sl + dw])
            e.barrier()

        with nc.Block() as block:
            for me in ENGS:
                def emit(eng, me=me):
                    dbg_slot[0] = 0
                    e = E(me, eng, sems)
                    upto = cfg.get("upto", "all")
                    p0_consts(e)
                    p1_weights(e)
                    ffn_phase(e, TA, x_in_tile, x_in_tile,
                              lambda t: x2_sc[128 * t:128 * t + 128, :])
                    if upto != "ffn1":
                        p3_qkv(e)
                    if upto not in ("ffn1", "qkv"):
                        p4_attn(e)
                        p5_outproj(e)
                        ffn_phase(e, TO,
                                  lambda t: x3_sc[128 * t:128 * t + 128, :],
                                  lambda t: x3_sc[128 * t:128 * t + 128, :],
                                  lambda t: out_d[128 * t:128 * t + 128, :])
                    if cfg.get("dump"):
                        for nm, srct in [("o_x2", x2_sc), ("o_qk", qk_sc),
                                         ("o_v", v_sc), ("o_x3", x3_sc),
                                         ("o_h", h_sc), ("o_dxq", dbg_xq),
                                         ("o_dst", dbg_st)]:
                            R = srct.tensor.shape[0] if hasattr(srct, "tensor") else 0
                            nrow = dump[nm].tensor.shape[0]
                            for r0 in range(0, nrow, 1024):
                                rr = min(1024, nrow - r0)
                                e.dma("gpsimd", "qd", dump[nm][r0:r0 + rr, :],
                                      srct[r0:r0 + rr, :])
                        e.barrier()
                    if me == "gpsimd":
                        eng.wait_ge(sems["ge"], e.cnt["ge"])
                        eng.wait_ge(sems["gy"], e.cnt["gy"])
                getattr(block, me)(emit)

    if SPLIT:
        nc.has_collectives = True
    return nc



_BUILT = {}


def _get_nc(key, cfg):
    if key not in _BUILT:
        _BUILT[key] = build(cfg)
    return _BUILT[key]


def make_in_maps(inputs, cfg):
    D, TO, TC = cfg["D"], cfg["TO"], cfg["TC"]
    QC = cfg["QC"]
    QS = QC // 128
    x = np.ascontiguousarray(np.asarray(inputs["L0_attn_hidden_states"],
                                        np.float32))
    wqkvT = np.ascontiguousarray(np.asarray(inputs["L0_attn_Wqkv_w"],
                                            np.float32).T)
    woT = np.ascontiguousarray(np.asarray(inputs["L0_attn_out_w"],
                                          np.float32).T)
    lrows = np.ascontiguousarray(np.stack([
        np.asarray(inputs["L0_attn_lambda_q1"], np.float32),
        np.asarray(inputs["L0_attn_lambda_k1"], np.float32),
        np.asarray(inputs["L0_attn_lambda_q2"], np.float32),
        np.asarray(inputs["L0_attn_lambda_k2"], np.float32)]))
    ident = np.eye(128, dtype=np.float32)
    bm = np.zeros((QS * 128, QC), np.float32)
    kl = np.arange(128)[:, None]
    ql = np.arange(QC)[None, :]
    for dj in range(QS):
        bm[128 * dj:128 * dj + 128] = np.where(kl <= ql - 128 * dj, 0.0,
                                               np.float32(NEG))
    zeros_ctx = np.zeros((TC, D), np.float32)
    wg_f = np.ascontiguousarray(np.asarray(inputs["L0_ffn_w_gate"], np.float32))
    wu_f = np.ascontiguousarray(np.asarray(inputs["L0_ffn_w_up"], np.float32))
    wd_f = np.ascontiguousarray(np.asarray(inputs["L0_ffn_w_down"], np.float32))
    shared = {
        "wqkvT": wqkvT, "woT": woT,
        "bqkv": np.asarray(inputs["L0_attn_Wqkv_b"], np.float32).reshape(1, -1),
        "bo": np.asarray(inputs["L0_attn_out_b"], np.float32).reshape(1, -1),
        "lrows_in": lrows,
        "sublnw_in": np.asarray(inputs["L0_attn_subln_w"], np.float32).reshape(1, -1),
        "identf_in": ident, "bandmask_in": bm,
    }
    SPLIT = bool(cfg.get("split"))
    F = cfg["F"]
    FT, DT = F // 128, D // 128
    KWG, KWD = (FT + 7) // 8, DT // 8

    def row_slices(w, rjs):
        out = np.zeros((len(rjs) * 128, w.shape[1]), np.float32)
        for i, rj in enumerate(rjs):
            if 128 * rj < w.shape[0]:
                out[128 * i:128 * i + 128] = w[128 * rj:128 * rj + 128]
        return out
    in_maps = []
    for c in range(8):
        b, hh = c // 2, c % 2
        m = dict(shared)
        if SPLIT:
            m["wg_slc"] = row_slices(wg_f, [c + 8 * k for k in range(KWG)])
            m["wu_slc"] = row_slices(wu_f, [c + 8 * k for k in range(KWG)])
            m["wd_slc"] = row_slices(wd_f, [2 * c + k for k in range(KWD)])
        else:
            m["wg"], m["wu"], m["wd"] = wg_f, wu_f, wd_f
        m["x_own"] = np.ascontiguousarray(x[b, TO * hh:TO * hh + TO])
        m["x_ctx"] = (np.ascontiguousarray(x[b, 0:TC]) if hh == 1 else zeros_ctx)
        m["ctxbias"] = np.array([[0.0 if hh == 1 else NEG]], np.float32)
        in_maps.append(m)
    return in_maps


def kernel(**inputs):
    cfg = dict(FULL)
    nc = _get_nc("full", cfg)
    in_maps = make_in_maps(inputs, cfg)
    res = run_bass_kernel_spmd(nc, in_maps, core_ids=list(range(8)))
    x = np.asarray(inputs["L0_attn_hidden_states"])
    B, S, D = x.shape
    out = np.empty((B, S, D), np.float32)
    for c in range(8):
        b, hh = c // 2, c % 2
        out[b, cfg["TO"] * hh:cfg["TO"] * hh + cfg["TO"]] = \
            res.results[c]["out"]
    return out


# version 23 (final)
# speedup vs baseline: 1.0152x; 1.0152x over previous
import math
import numpy as np
import concourse.bass as bass
import concourse.mybir as mybir
from concourse.bass_utils import run_bass_kernel_spmd

F32 = mybir.dt.float32
BF16 = mybir.dt.bfloat16
I32 = mybir.dt.int32
AF = mybir.ActivationFunctionType
ALU = mybir.AluOpType
AX = mybir.AxisListType

MAGIC = float(np.float32(1.5 * 2 ** 23))
EPS = 1e-5
LAMBDA_INIT = float(np.float32(0.8 - 0.6 * math.exp(-0.3 * 1)))
ALPHA = float(np.float32(128 ** -0.5))
NEG = -1e30

FULL = dict(D=2048, F=5504, TO=2048, TC=2048, PH=8, NW=512,
            ASC=2048, BSC=512, QC=256)

ENGS = ("sync", "vector", "scalar", "tensor", "gpsimd")
_SEMS = ("ls", "lv", "lc", "lt", "lq",
         "ga", "gb", "gc", "gd", "ge",
         "ha", "hb", "hc", "hd", "he", "hf", "hg",
         "qa", "qb", "qc", "qd", "qe", "gx", "gy", "wga", "wgb", "rb",
         "ha2", "ls2", "he2", "qa2", "ob2", "od2",
         "aa", "ab", "ac", "ad", "ae", "af", "ag", "ao", "aq",
         "oa", "ob", "oc", "od",
         "fs", "fv", "fc", "ft", "fg")
_FENCE = {"sync": "fs", "vector": "fv", "scalar": "fc",
          "tensor": "ft", "gpsimd": "fg"}


class E:

    def __init__(self, me, eng, sems):
        self.me = me
        self.eng = eng
        self.sems = sems
        self.cnt = {k: 0 for k in sems}
        self.base = {k: 0 for k in sems}
        self.last = None
        self.last_compute = None
        self.dirty = {x: False for x in ENGS}
        self.cdirty = {x: False for x in ENGS}
        self.dma_sems = {x: set() for x in ENGS}
        self.last_op_sem = {x: None for x in ENGS}

    def op(self, owner, f, sem=None, amt=1):
        if sem is not None:
            self.cnt[sem] += amt
            self.last_op_sem[owner] = (sem, self.cnt[sem])
        else:
            self.last_op_sem[owner] = None
        self.dirty[owner] = True
        self.cdirty[owner] = True
        if self.me == owner:
            ins = f(self.eng)
            self.last = ins
            self.last_compute = ins
            if sem is not None:
                ins.then_inc(self.sems[sem], amt)
            return ins

    def inc_last(self, owner, sem, amt=1):
        self.cnt[sem] += amt
        self.last_op_sem[owner] = (sem, self.cnt[sem])
        if self.me == owner:
            self.last.then_inc(self.sems[sem], amt)

    def dma(self, owner, sem, out, in_):
        self.cnt[sem] += 16
        self.dirty[owner] = True
        self.dma_sems[owner].add(sem)
        if self.me == owner:
            ins = self.eng.dma_start(out=out, in_=in_)
            self.last = ins
            ins.then_inc(self.sems[sem], 16)
            return ins

    def wait(self, owner, sem, local=None):
        if local is None:
            val = self.cnt[sem]
            if val == self.base[sem]:
                return
        else:
            if local <= 0:
                return
            val = self.base[sem] + local
            assert val <= self.cnt[sem], (sem, val, self.cnt[sem])
        if self.me == owner:
            self.eng.wait_ge(self.sems[sem], val)

    def loc(self, sem):
        return self.cnt[sem] - self.base[sem]

    def barrier(self):
        used = sorted(set().union(*self.dma_sems.values()))
        for x in ENGS:
            for s in used:
                self.wait(x, s)
            self.dma_sems[x] = set()
        fences = []
        for x in ENGS:
            if self.cdirty[x]:
                los = self.last_op_sem[x]
                if los is not None:
                    fences.append(los)
                else:
                    fs = _FENCE[x]
                    self.cnt[fs] += 1
                    fences.append((fs, self.cnt[fs]))
                    if self.me == x:
                        assert self.last_compute is not None
                        self.last_compute.then_inc(self.sems[fs], 1)
            self.cdirty[x] = False
            self.dirty[x] = False
            self.last_op_sem[x] = None
        for x in ENGS:
            for (s, v) in fences:
                if self.me == x:
                    self.eng.wait_ge(self.sems[s], v)
        self.base = dict(self.cnt)
        self.last = None
        self.last_compute = None


def build(cfg):
    D, F, TO, TC = cfg["D"], cfg["F"], cfg["TO"], cfg["TC"]
    PH, NW = cfg["PH"], cfg["NW"]
    ASC, BSC, QC = cfg["ASC"], cfg["BSC"], cfg["QC"]
    KVPH = PH // 2
    QF, KF, VF = PH * 256, KVPH * 256, KVPH * 256
    QKVF = QF + KF + VF
    DT, FT = D // 128, F // 128
    TA = TC + TO
    KT = TA // 128
    QS = QC // 128
    VEXT = 257
    FCH = [(i * NW, min(NW, F - i * NW)) for i in range((F + NW - 1) // NW)]
    DCH = [(i * NW, min(NW, D - i * NW)) for i in range((D + NW - 1) // NW)]
    NFC = len(FCH)
    G3 = (FT + 2) // 3
    W3 = [(i * G3, min(G3, FT - i * G3)) for i in range(3) if i * G3 < FT]
    NP = len(W3)
    NBK = (DT + 3) // 4

    nc = bass.Bass("TRN2", target_bir_lowering=False)

    def din(name, shape, dt=F32):
        return nc.dram_tensor(name, shape, dt, kind="ExternalInput").ap()

    x_own = din("x_own", [TO, D])
    x_ctx = din("x_ctx", [TC, D])
    SPLIT = bool(cfg.get("split"))
    KWG = (FT + 7) // 8
    KWD = DT // 8
    if SPLIT:
        wg_d = din("wg_slc", [KWG * 128, D])
        wu_d = din("wu_slc", [KWG * 128, D])
        wd_d = din("wd_slc", [KWD * 128, F])
    else:
        wg_d = din("wg", [F, D])
        wu_d = din("wu", [F, D])
        wd_d = din("wd", [D, F])
    wqkvT_d = din("wqkvT", [D, QKVF])
    woT_d = din("woT", [QF, D])
    bqkv_d = din("bqkv", [1, QKVF])
    bo_d = din("bo", [1, D])
    lrows_d = din("lrows_in", [4, 128])
    sublnw_d = din("sublnw_in", [1, 256])
    identf_d = din("identf_in", [128, 128])
    bandmask_d = din("bandmask_in", [QS * 128, QC])
    ctxbias_d = din("ctxbias", [1, 1])

    out_d = nc.dram_tensor("out", [TO, D], F32, kind="ExternalOutput").ap()

    def scr(name, shape, dt):
        return nc.dram_tensor(name, shape, dt, kind="Internal").ap()

    FPAD = (KWG * 8 * 128 + 128) if SPLIT else F
    wgq_sc = scr("wgq_sc", [D, FPAD], BF16)
    wuq_sc = scr("wuq_sc", [D, FPAD], BF16)
    wdq_sc = scr("wdq_sc", [F, D], BF16)
    if SPLIT:
        wgq_loc = scr("wgq_loc", [D, KWG * 128], BF16)
        wuq_loc = scr("wuq_loc", [D, KWG * 128], BF16)
        wdq_loc = scr("wdq_loc", [F, KWD * 128], BF16)
    wqkv_bsc = scr("wqkv_bsc", [D, QKVF], BF16)
    woT_bsc = scr("woT_bsc", [QF, D], BF16)
    h_sc = scr("h_sc", [TA, F], F32)
    x2_sc = scr("x2_sc", [TA, D], F32)
    x3_sc = scr("x3_sc", [TO, D], F32)
    qk_sc = scr("qk_sc", [QF + KF, TA], BF16)
    v_sc = scr("v_sc", [TA, VF], BF16)
    dump = {}
    if cfg.get("dump"):
        for nm, shape, dt in [("o_x2", [TA, D], F32), ("o_qk", [QF + KF, TA], BF16),
                              ("o_v", [TA, VF], BF16), ("o_x3", [TO, D], F32),
                              ("o_h", [TA, F], F32),
                              ("o_dxq", [((TA + TO) // cfg["ASC"] + 4) * 128,
                                         (D // 128) * cfg["ASC"]], BF16),
                              ("o_dst", [((TA + TO) // cfg["ASC"] + 4) * 128,
                                         cfg["ASC"] // 128], F32)]:
            dump[nm] = nc.dram_tensor(nm, shape, dt, kind="ExternalOutput").ap()
    dbg_xq = (scr("dbg_xq", [((TA + TO) // cfg["ASC"] + 4) * 128, (D // 128) * cfg["ASC"]], BF16)
              if cfg.get("dbgx") else None)
    dbg_st = (scr("dbg_st", [((TA + TO) // cfg["ASC"] + 4) * 128, cfg["ASC"] // 128], F32)
              if cfg.get("dbgx") else None)

    import contextlib
    es = contextlib.ExitStack()

    def sb(name, p, f, dt=F32):
        return es.enter_context(nc.sbuf_tensor(name, [p, f], dt))

    with es:
        big_a = sb("big_a", 128, max(DT * ASC, 2 * PH * TO), BF16)
        big_b = sb("big_b", 128, max(FT * BSC, 2 * TA + 2 * TO + KT * VEXT, 20480), BF16)
        wbuf = sb("wbuf", 128,
                  max(2 * DT * NW, 2 * G3 * NW, 2 * 2 * PH * NW), BF16)
        xbuf = sb("xbuf", 128, max(2 * D, 4096), F32)
        hbuf = sb("hbuf", 128, max(D, 4 * NW), F32)
        ebuf = sb("ebuf", 128, max(2 * max(NW, QC, D // 2), 4 * 512), BF16)
        i32b = sb("i32b", 128, NW, I32)
        i32c = sb("i32c", 128, NW, I32)
        obuf = sb("obuf", 128, 6 * 256, F32)
        gbuf = sb("gbuf", 128, 2 * 128, F32)
        wqb = sb("wqb", 128, 2 * 128, BF16)
        wtp = sb("wtp", 128, 2 * 128, BF16)
        identf = sb("identf", 128, 128, F32)
        identb = sb("identb", 128, 128, BF16)
        bandm = sb("bandm", 128, QS * QC, F32)
        swtile = sb("swtile", 128, 256, F32)
        lamncol = sb("lamncol", 128, 1, F32)
        biascol = sb("biascol", 128, 6, F32)
        mtab = sb("mtab", 128, 32, F32)
        onescol = sb("onescol", 128, 1, BF16)
        onescf = sb("onescf", 128, 1, F32)
        ones1 = sb("ones1", 1, 128, BF16)
        onesrow = sb("onesrow", 1, NW, BF16)
        bqkvrow = sb("bqkvrow", 1, QKVF, BF16)
        borow = sb("borow", 1, D, BF16)
        lrow = sb("lrow", 1, 512, F32)
        sublnw = sb("sublnw", 1, 256, F32)
        srow = sb("srow", 1, 128, F32)
        bn6 = sb("bn6", 128, 6 * max(1, (D + 511) // 512), F32)
        mv = sb("mv", 128, 2, F32)
        rrc = sb("rrc", 128, 2, F32)
        tmpc = sb("tmpc", 128, 8, F32)
        sinv_a = sb("sinv_a", 128, ASC // 128, F32)
        hmax_a = sb("hmax_a", 128, ASC // 128, F32)
        ch_a = sb("ch_a", 128, ASC // 128, F32)
        hs_a = sb("hs_a", 128, ASC // 128, F32)
        wsc = sb("wsc", 128, 4, F32)
        epsc = sb("epsc", 128, 4, F32)
        ps = [es.enter_context(nc.psum_tensor("p%d" % i, [128, 512], F32))
              for i in range(8)]

        def psbf(i):
            return ps[i][:].bitcast(BF16)
        sems = {s: es.enter_context(nc.semaphore(s)) for s in _SEMS}

        dbg_slot = [0]

        def x_in_tile(t):
            r = t * 128
            if r < TC:
                return x_ctx[r:r + 128, :]
            return x_own[r - TC:r - TC + 128, :]

        def p0_consts(e: E):
            e.dma("sync", "ls", identf[:], identf_d)
            e.dma("sync", "ls",
                  bandm[:, 0:QS * QC].rearrange("p (a q) -> p a q", a=QS),
                  bandmask_d.rearrange("(a p) q -> p a q", a=QS))
            e.dma("sync", "ls", lrow[:], lrows_d.rearrange("a b -> (a b)").unsqueeze(0))
            e.dma("sync", "ls", sublnw[:], sublnw_d)
            e.dma("sync", "ls", srow[0:1, 8:9], ctxbias_d)
            for (dst, srcd, L) in ((bqkvrow, bqkv_d, QKVF), (borow, bo_d, D)):
                for c0 in range(0, L, D):
                    w = min(D, L - c0)
                    e.wait("sync", "lv")
                    e.dma("sync", "ls", xbuf[0:1, 0:w], srcd[0:1, c0:c0 + w])
                    e.wait("vector", "ls")
                    e.op("vector", lambda g, dst=dst, c0=c0, w=w: g.tensor_copy(
                        dst[0:1, c0:c0 + w], xbuf[0:1, 0:w]), sem="lv")
            e.wait("vector", "ls")
            e.op("vector", lambda g: g.memset(epsc[:, 0:1], EPS))
            e.op("vector", lambda g: g.memset(epsc[:, 1:2], 1e-38))
            e.op("vector", lambda g: g.memset(epsc[:, 2:3], MAGIC))
            e.op("vector", lambda g: g.memset(epsc[:, 3:4], -MAGIC))
            e.op("vector", lambda g: g.memset(onescol[:], 1.0))
            e.op("vector", lambda g: g.memset(onescf[:], 1.0))
            e.op("vector", lambda g: g.memset(ones1[:], 1.0))
            e.op("vector", lambda g: g.memset(onesrow[:], 1.0))
            e.op("vector", lambda g: g.tensor_copy(identb[:], identf[:]))
            e.op("vector", lambda g: g.tensor_mul(lrow[0:1, 0:128],
                                                  lrow[0:1, 0:128],
                                                  lrow[0:1, 128:256]))
            e.op("vector", lambda g: g.tensor_mul(lrow[0:1, 256:384],
                                                  lrow[0:1, 256:384],
                                                  lrow[0:1, 384:512]), sem="lv")
            e.wait("vector", "lv")
            e.op("vector", lambda g: g.reduce_sum(srow[0:1, 0:1], lrow[0:1, 0:128], AX.X))
            e.op("vector", lambda g: g.reduce_sum(srow[0:1, 1:2], lrow[0:1, 256:384], AX.X),
                 sem="lv")
            e.wait("scalar", "lv")
            e.op("scalar", lambda g: g.activation(srow[0:1, 2:4], srow[0:1, 0:2],
                                                  AF.Exp), sem="lc")
            e.wait("vector", "lc")
            e.op("vector", lambda g: g.tensor_sub(srow[0:1, 4:5], srow[0:1, 2:3],
                                                  srow[0:1, 3:4]), sem="lv")
            e.wait("vector", "lv")
            e.op("vector", lambda g: g.tensor_scalar(srow[0:1, 4:5], srow[0:1, 4:5],
                                                     LAMBDA_INIT, -1.0,
                                                     ALU.add, ALU.mult))
            e.op("vector", lambda g: g.tensor_scalar_mul(sublnw[0:1, :],
                                                         sublnw[0:1, :],
                                                         1.0 - LAMBDA_INIT), sem="lv")
            e.wait("vector", "lv")
            e.op("vector", lambda g: g.tensor_copy(ebuf[0:1, 4:5], srow[0:1, 4:5]))
            e.op("vector", lambda g: g.tensor_copy(ebuf[0:1, 8:9], srow[0:1, 8:9]))
            e.op("vector", lambda g: g.tensor_copy(ebuf[0:1, 16:16 + 256],
                                                   sublnw[0:1, :]), sem="lv")
            e.wait("tensor", "lv")
            e.op("tensor", lambda g: g.matmul(ps[0][:, 0:1], ones1[:], ebuf[0:1, 4:5],
                                              start=True, stop=True))
            e.op("tensor", lambda g: g.matmul(ps[0][:, 4:5], ones1[:], ebuf[0:1, 8:9],
                                              start=True, stop=True))
            e.op("tensor", lambda g: g.matmul(ps[1][:, 0:256], ones1[:],
                                              ebuf[0:1, 16:16 + 256],
                                              start=True, stop=True), sem="lt")
            e.wait("vector", "lt")
            e.op("vector", lambda g: g.tensor_copy(lamncol[:], ps[0][:, 0:1]))
            e.op("vector", lambda g: g.tensor_copy(swtile[:], ps[1][:, 0:256]))
            e.op("vector", lambda g: g.tensor_copy(biascol[:, 4:5], ps[0][:, 4:5]),
                 sem="lv")
            e.barrier()

        def p1_weights(e: E):
            def quant(w_in, w_out, O, I):
                G = I // 128
                OT = O // 128
                wov = w_out.rearrange("(g p) c -> g p c", g=G)
                CH = [(c * 16, min(16, G - c * 16)) for c in range((G + 15) // 16)]
                i32u = big_b[:, 0:8192].bitcast(I32)
                dbw = big_b[:, 8192:16384].bitcast(F32)
                wqrow = big_b[:, 16384:20480]
                for rj in range(OT):
                    for (g0, gc) in CH:
                        W = gc * 128
                        j = (e.loc("ga") + e.loc("gx")) // 16
                        sl = j % 2
                        veng = "vector"
                        qsem = "gb"
                        lsem = "ga" if sl == 0 else "gx"
                        e.wait("sync", qsem, 6 * (j - 1))
                        e.wait("sync", "lv", j - 1)
                        e.dma("sync", lsem, xbuf[:, 2048 * sl:2048 * sl + W],
                              w_in[128 * rj:128 * rj + 128,
                                   128 * g0:128 * g0 + W])
                        wr = xbuf[:, 2048 * sl:2048 * sl + W]
                        wv = wr.rearrange("p (g c) -> p g c", g=gc)
                        ui = i32u[:, 2048 * sl:2048 * sl + W]
                        dd = dbw[:, 2048 * sl:2048 * sl + W]
                        ci32 = dd.bitcast(I32)
                        uf = ui.bitcast(F32)
                        sb_ = gbuf[:, 128 * sl:128 * sl + 96]
                        e.wait("vector", lsem, 16 * (j // 2 + 1))
                        e.op("vector", lambda g, wv=wv, sb_=sb_, gc=gc:
                             g.tensor_reduce(sb_[:, 0:gc], wv, AX.X, ALU.add,
                                             apply_absolute_value=True),
                             sem="ag")
                        e.wait("vector", "ag")
                        e.op("vector", lambda g, sb_=sb_, gc=gc: g.tensor_scalar(
                            sb_[:, 32:32 + gc], sb_[:, 0:gc], 1.0 / 128.0,
                            EPS, ALU.mult, ALU.add), sem="ag")
                        e.wait("vector", "ag")
                        e.op("vector", lambda g, sb_=sb_, gc=gc: g.reciprocal(
                            sb_[:, 64:64 + gc], sb_[:, 32:32 + gc]), sem="ag")
                        e.wait(veng, "ag", 3 * (j + 1))
                        for gi_ in range(gc):
                            e.op(veng, lambda g, gi_=gi_, wv=wv, sb_=sb_:
                                 g.tensor_scalar(
                                     wv[:, gi_, :], wv[:, gi_, :],
                                     sb_[:, 64 + gi_:64 + gi_ + 1], 1.0,
                                     ALU.mult, ALU.min),
                                 sem=(qsem if gi_ == gc - 1 else None))
                        e.wait(veng, qsem)
                        e.op(veng, lambda g, wr=wr: g.tensor_scalar(
                            wr, wr, -1.0, MAGIC, ALU.max, ALU.add), sem=qsem)
                        e.wait(veng, qsem)
                        e.op(veng, lambda g, ui=ui, wr=wr:
                             g.tensor_single_scalar(
                                 ui, wr, MAGIC, ALU.subtract), sem=qsem)
                        e.wait(veng, qsem)
                        e.op(veng, lambda g, dd=dd, wr=wr, ui=ui:
                             g.scalar_tensor_tensor(
                                 dd, wr, -MAGIC, ui,
                                 ALU.add, ALU.subtract), sem=qsem)
                        e.wait(veng, qsem)
                        e.op(veng, lambda g, ci32=ci32, dd=dd:
                             g.tensor_single_scalar(
                                 ci32, dd, 2.0, ALU.mult), sem=qsem)
                        e.wait(veng, qsem)
                        e.op(veng, lambda g, uf=uf, ui=ui, ci32=ci32:
                             g.tensor_add(uf, ui, ci32), sem=qsem)
                        e.wait("scalar", qsem, 6 * (j + 1))
                        e.wait("scalar", "gc")
                        qfv = uf.rearrange("p (g c) -> p g c", g=gc)
                        qv = wqrow[:, 2048 * sl:2048 * sl + W].rearrange(
                            "p (g c) -> p g c", g=gc)
                        for gi_ in range(gc):
                            e.op("scalar", lambda g, gi_=gi_, qfv=qfv, qv=qv,
                                 sb_=sb_:
                                 g.activation(
                                     qv[:, gi_, :], qfv[:, gi_, :], AF.Copy,
                                     scale=sb_[:, 32 + gi_:32 + gi_ + 1]),
                                 sem=("lv" if gi_ == gc - 1 else None))
                        e.wait("tensor", "lv", j + 1)
                        nbk = (gc + 3) // 4
                        for bk in range(nbk):
                            nblk = min(4, gc - 4 * bk)
                            bidx = e.loc("gc")
                            e.wait("tensor", "gd", bidx - 1)
                            for bi in range(nblk):
                                e.op("tensor", lambda g, sl=sl, bk=bk, bi=bi,
                                     bidx=bidx:
                                     g.transpose(
                                         psbf(4 + (bidx % 2))[:, 128 * bi:128 * bi + 128],
                                         wqrow[:, 2048 * sl + 512 * bk + 128 * bi:
                                              2048 * sl + 512 * bk + 128 * bi + 128],
                                         identb[:]),
                                     sem=("gc" if bi == nblk - 1 else None))
                            e.wait("scalar", "gc", bidx + 1)
                            esl2 = bidx % 2
                            ssem = "ge" if esl2 == 0 else "gy"
                            e.wait("scalar", ssem, 16 * (e.loc("gd") // 2))
                            e.op("scalar", lambda g, bidx=bidx, nblk=nblk,
                                 esl2=esl2:
                                 g.activation(
                                     ebuf[:, NW * esl2:NW * esl2 + 128 * nblk],
                                     psbf(4 + (bidx % 2))[:, 0:128 * nblk],
                                     AF.Copy), sem="gd")
                            e.wait("gpsimd", "gd", e.loc("gd"))
                            e.dma("gpsimd", ssem,
                                  wov[g0 + 4 * bk:g0 + 4 * bk + nblk, :,
                                      128 * rj:128 * rj + 128]
                                  .transpose([1, 0, 2]),
                                  ebuf[:, NW * esl2:NW * esl2 + 128 * nblk]
                                  .rearrange("p (b c) -> p b c", b=nblk))
            if SPLIT:
                quant(wg_d, wgq_loc, KWG * 128, D)
                quant(wu_d, wuq_loc, KWG * 128, D)
                quant(wd_d, wdq_loc, KWD * 128, F)
            else:
                quant(wg_d, wgq_sc, F, D)
                quant(wu_d, wuq_sc, F, D)
                quant(wd_d, wdq_sc, D, F)
            e.barrier()
            if SPLIT:
                for (loc, glob, pat, kk) in (
                        (wgq_loc, wgq_sc, "R d (k c) -> S d (k R c)", KWG),
                        (wuq_loc, wuq_sc, "R d (k c) -> S d (k R c)", KWG),
                        (wdq_loc, wdq_sc, "R f (k c) -> S f (R k c)", KWD)):
                    R = loc.shape[0]
                    gw = kk * 8 * 128
                    for r0 in range(0, R, 128):
                        e.cnt["rb"] += 16
                        if e.me == "gpsimd":
                            e.eng.remote_dma_broadcast_einops(
                                glob[r0:r0 + 128, 0:gw],
                                loc[r0:r0 + 128, :], sems["rb"],
                                None, repeat=pat, sizes={"k": kk, "c": 128})
                e.wait("sync", "rb")

            def cast(w_in, w_out, R, C):
                chs = [(i * NW, min(NW, C - i * NW))
                       for i in range((C + NW - 1) // NW)]
                for r in range(R // 128):
                    for ji, (c0, cw) in enumerate(chs):
                        j = (e.loc("ha") + e.loc("ha2") + e.loc("qa")
                             + e.loc("qa2")) // 16
                        sl = j % 4
                        e.wait("sync", "hb", j - 3)
                        lsem = ["ha", "ha2", "qa", "qa2"][sl]
                        e.dma("sync", lsem, hbuf[:, NW * sl:NW * sl + cw],
                              w_in[128 * r:128 * r + 128, c0:c0 + cw])
                        ssem = ["qd", "qe", "ge", "gy"][sl]
                        e.wait("vector", lsem, 16 * (j // 4 + 1))
                        e.wait("vector", ssem, 16 * (j // 4))
                        e.op("vector", lambda g, sl=sl, cw=cw: g.tensor_copy(
                            ebuf[:, NW * sl:NW * sl + cw],
                            hbuf[:, NW * sl:NW * sl + cw]), sem="hb")
                        e.wait("gpsimd", "hb", j + 1)
                        e.dma("gpsimd", ssem,
                              w_out[128 * r:128 * r + 128, c0:c0 + cw],
                              ebuf[:, NW * sl:NW * sl + cw])
            cast(wqkvT_d, wqkv_bsc, D, QKVF)
            cast(woT_d, woT_bsc, QF, D)
            e.barrier()

        def ln_quant_sweep(e: E, a, AT, src_tile_fn, want_quant):
            LVT = (10 + 4 * NBK) if want_quant else 3
            XCOFF = 3
            for t in range(AT):
                tg = a * AT + t
                sl = t % 2
                e.wait("sync", "lv", (t - 2) * LVT + XCOFF)
                lsem = "ls" if sl == 0 else "ls2"
                e.dma("sync", lsem, xbuf[:, D * sl:D * sl + D], src_tile_fn(tg))
                e.wait("vector", lsem, 16 * (t // 2 + 1))
                nch = (D + 511) // 512
                for q in range(nch):
                    cw = min(512, D - 512 * q)
                    e.op("vector", lambda g, sl=sl, q=q, cw=cw: g.bn_stats(
                        bn6[:, 6 * q:6 * q + 6],
                        xbuf[:, D * sl + 512 * q:D * sl + 512 * q + cw]),
                        sem=("lv" if q == nch - 1 else None))
                e.wait("vector", "lv")
                e.op("vector", lambda g, nch=nch: g.bn_aggr(
                    mv[:, 0:2], bn6[:, 0:6 * nch]), sem="lv")
                e.wait("scalar", "lv", t * LVT + 2)
                e.op("scalar", lambda g: g.activation(
                    tmpc[:, 0:1], mv[:, 1:2], AF.Ln, bias=epsc[:, 0:1]), sem="lc")
                e.wait("scalar", "lc")
                e.op("scalar", lambda g: g.activation(
                    rrc[:, 0:1], tmpc[:, 0:1], AF.Exp, scale=-0.5),
                    sem=(None if want_quant else "lc"))
                if want_quant:
                    e.op("scalar", lambda g: g.activation(
                        rrc[:, 1:2], tmpc[:, 0:1], AF.Exp, scale=0.5), sem="lc")
                e.wait("vector", "lc", 2 * t + 2)
                e.wait("vector", "lt", t * NBK)
                if want_quant:
                    e.op("vector", lambda g, sl=sl: g.tensor_scalar(
                        hbuf[:, 0:D], xbuf[:, D * sl:D * sl + D], mv[:, 0:1],
                        None, ALU.subtract), sem="lv")
                    e.wait("vector", "lv")
                    e.op("vector", lambda g: g.tensor_reduce(
                        tmpc[:, 1:2], hbuf[:, 0:D], AX.X, ALU.max,
                        apply_absolute_value=True), sem="lv")
                    e.wait("vector", "lv")
                    e.op("vector", lambda g: g.tensor_scalar_mul(
                        tmpc[:, 2:3], rrc[:, 1:2], EPS), sem="lv")
                    e.wait("vector", "lv")
                    e.op("vector", lambda g: g.tensor_tensor(
                        tmpc[:, 3:4], tmpc[:, 1:2], tmpc[:, 2:3], ALU.max),
                        sem="lv")
                    e.wait("vector", "lv")
                    e.op("vector", lambda g: g.reciprocal(tmpc[:, 4:5], tmpc[:, 3:4]))
                    e.op("vector", lambda g: g.tensor_mul(
                        tmpc[:, 5:6], tmpc[:, 1:2], rrc[:, 0:1]), sem="lv")
                    e.wait("vector", "lv")
                    e.op("vector", lambda g: g.tensor_scalar_mul(
                        tmpc[:, 4:5], tmpc[:, 4:5], 127.0))
                    e.op("vector", lambda g, t=t: g.tensor_scalar(
                        sinv_a[:, t:t + 1], tmpc[:, 5:6], EPS, 1.0 / 127.0,
                        ALU.max, ALU.mult), sem="lv")
                    e.wait("vector", "lv")
                    e.op("vector", lambda g: g.tensor_scalar(
                        hbuf[:, 0:D], hbuf[:, 0:D], tmpc[:, 4:5], MAGIC,
                        ALU.mult, ALU.add), sem="lv")
                    e.wait("vector", "lv")
                    e.op("vector", lambda g: g.tensor_single_scalar(
                        hbuf[:, 0:D], hbuf[:, 0:D], MAGIC, ALU.subtract),
                        sem="lv")
                else:
                    e.op("vector", lambda g, sl=sl: g.tensor_scalar(
                        ebuf[:, 0:D], xbuf[:, D * sl:D * sl + D], mv[:, 0:1],
                        rrc[:, 0:1], ALU.subtract, ALU.mult), sem="lv")
                e.wait("tensor", "lv", t * LVT + (10 if want_quant else 3))
                xv = big_a[:, 0:DT * ASC].rearrange("p (d t) -> p d t", d=DT)
                for bk in range(NBK):
                    nblk = min(4, DT - 4 * bk)
                    gi = t * NBK + bk
                    e.wait("tensor", "lq", gi - 1)
                    for bi in range(nblk):
                        d = 4 * bk + bi
                        src = hbuf if want_quant else ebuf
                        idd = identf if want_quant else identb
                        if want_quant:
                            e.op("tensor", lambda g, d=d, bk=bk, bi=bi, src=src,
                                 idd=idd:
                                 g.transpose(ps[bk % 2][:, 128 * bi:128 * bi + 128],
                                             src[:, 128 * d:128 * d + 128], idd[:]),
                                 sem=("lt" if bi == nblk - 1 else None))
                        else:
                            e.op("tensor", lambda g, d=d, bk=bk, bi=bi, src=src,
                                 idd=idd:
                                 g.transpose(psbf(bk % 2)[:, 128 * bi:128 * bi + 128],
                                             src[:, 128 * d:128 * d + 128], idd[:]),
                                 sem=("lt" if bi == nblk - 1 else None))
                    if want_quant:
                        e.wait("vector", "lt", gi + 1)
                        e.wait("vector", "lq", gi)
                        e.op("vector", lambda g, bk=bk, nblk=nblk: g.tensor_copy(
                            i32b[:, 0:128 * nblk], ps[bk % 2][:, 0:128 * nblk]),
                            sem="lv")
                        e.wait("vector", "lv")
                        e.op("vector", lambda g, bk=bk, nblk=nblk: g.tensor_sub(
                            obuf[:, 0:128 * nblk], ps[bk % 2][:, 0:128 * nblk],
                            i32b[:, 0:128 * nblk]), sem="lv")
                        e.wait("vector", "lv")
                        e.op("vector", lambda g, nblk=nblk: g.tensor_scalar_mul(
                            obuf[:, 0:128 * nblk], obuf[:, 0:128 * nblk], 2.0),
                            sem="lv")
                        e.wait("vector", "lv")
                        e.op("vector", lambda g, nblk=nblk: g.tensor_copy(
                            i32c[:, 0:128 * nblk], obuf[:, 0:128 * nblk]),
                            sem="lv")
                        e.wait("vector", "lv")
                        e.wait("vector", "lq", gi)
                        e.op("vector", lambda g, bk=bk, nblk=nblk, t=t:
                             g.tensor_add(
                                 xv[:, 4 * bk:4 * bk + nblk,
                                    128 * t:128 * t + 128],
                                 i32b[:, 0:128 * nblk].rearrange(
                                     "p (d t) -> p d t", d=nblk),
                                 i32c[:, 0:128 * nblk].rearrange(
                                     "p (d t) -> p d t", d=nblk)), sem="lq")
                    else:
                        e.wait("scalar", "lt", gi + 1)
                        e.op("scalar", lambda g, bk=bk, nblk=nblk, t=t: g.activation(
                            xv[:, 4 * bk:4 * bk + nblk, 128 * t:128 * t + 128],
                            psbf(bk % 2)[:, 0:128 * nblk].rearrange(
                                "p (d t) -> p d t", d=nblk),
                            AF.Copy), sem="lq")

        def ffn_phase(e: E, Ttok, in_tile_fn, resid_tile_fn, out_tile_fn):
            nA = Ttok // ASC
            AT = ASC // 128
            BT = BSC // 128
            xqv = big_a[:, 0:DT * ASC].rearrange("p (d t) -> p d t", d=DT)
            hqv = big_b[:, 0:FT * BSC].rearrange("p (f t) -> p f t", f=FT)
            for a in range(nA):
                ln_quant_sweep(e, a, AT, in_tile_fn, want_quant=True)
                e.barrier()
                if dbg_xq is not None:
                    slot = dbg_slot[0]
                    dbg_slot[0] += 1
                    e.dma("gpsimd", "qd", dbg_xq[128 * slot:128 * slot + 128, :],
                          big_a[:, 0:DT * ASC])
                    e.dma("gpsimd", "qd",
                          dbg_st[128 * slot:128 * slot + 128, :],
                          sinv_a[:, 0:AT])
                    e.barrier()
                e.op("vector", lambda g: g.memset(hmax_a[:, 0:AT], 0.0), sem="lv")
                for ci, (c0, cw) in enumerate(FCH):
                    DQ = max(1, DT // 4)
                    e.wait("sync", "gb", 2 * ci * AT - 1)
                    for d1 in range(0, DT, DQ):
                        e.dma("sync", "wga",
                              wbuf[:, NW * d1:NW * (d1 + DQ)].rearrange(
                                  "p (d c) -> p d c", d=DQ)[:, :, 0:cw],
                              wgq_sc[128 * d1:128 * (d1 + DQ), c0:c0 + cw]
                              .rearrange("(d p) c -> p d c", d=DQ))
                    e.wait("sync", "gb", 2 * ci * AT)
                    for d1 in range(0, DT, DQ):
                        e.dma("sync", "wgb",
                              wbuf[:, DT * NW + NW * d1:
                                   DT * NW + NW * (d1 + DQ)].rearrange(
                                  "p (d c) -> p d c", d=DQ)[:, :, 0:cw],
                              wuq_sc[128 * d1:128 * (d1 + DQ), c0:c0 + cw]
                              .rearrange("(d p) c -> p d c", d=DQ))
                    e.wait("tensor", "wga", 64 * (ci + 1))
                    e.wait("tensor", "wgb", 64 * (ci + 1))
                    for t in range(AT):
                        n = ci * AT + t
                        pbg = 2 + (t % 2)
                        e.wait("tensor", "gc", n - 1)
                        for dti in range(DT):
                            e.op("tensor",
                                 lambda g, pbg=pbg, dti=dti, t=t, cw=cw:
                                 g.matmul(ps[pbg][:, 0:cw],
                                          xqv[:, dti, 128 * t:128 * t + 128],
                                          wbuf[:, NW * dti:NW * dti + cw],
                                          start=(dti == 0),
                                          stop=(dti == DT - 1)),
                                 sem=("gb" if dti == DT - 1 else None))
                        pbu = 4 + (t % 2)
                        e.wait("tensor", "gd", 3 * n - 5)
                        for dti in range(DT):
                            e.op("tensor",
                                 lambda g, pbu=pbu, dti=dti, t=t, cw=cw:
                                 g.matmul(ps[pbu][:, 0:cw],
                                          xqv[:, dti, 128 * t:128 * t + 128],
                                          wbuf[:, DT * NW + NW * dti:
                                               DT * NW + NW * dti + cw],
                                          start=(dti == 0),
                                          stop=(dti == DT - 1)),
                                 sem=("gb" if dti == DT - 1 else None))
                        e.wait("scalar", "gb", 2 * n + 1)
                        e.wait("scalar", "gd", 3 * n - 5)
                        e.op("scalar", lambda g, pbg=pbg, t=t, cw=cw:
                             g.activation(
                                 hbuf[:, NW * (t % 2):NW * (t % 2) + cw],
                                 ps[pbg][:, 0:cw], AF.Silu,
                                 scale=sinv_a[:, t:t + 1]), sem="gc")
                        e.wait("vector", "gb", 2 * n + 2)
                        e.wait("vector", "gc", n + 1)
                        e.wait("vector", "ge" if n % 2 == 0 else "gy",
                               16 * (n // 2))
                        e.op("vector", lambda g, pbu=pbu, t=t, cw=cw:
                             g.scalar_tensor_tensor(
                                 hbuf[:, NW * (2 + t % 2):NW * (2 + t % 2) + cw],
                                 ps[pbu][:, 0:cw], sinv_a[:, t:t + 1],
                                 hbuf[:, NW * (t % 2):NW * (t % 2) + cw],
                                 ALU.mult, ALU.mult), sem="gd")
                        e.wait("vector", "gd")
                        e.op("vector", lambda g, t=t, cw=cw: g.tensor_reduce(
                            tmpc[:, 6:7],
                            hbuf[:, NW * (2 + t % 2):NW * (2 + t % 2) + cw],
                            AX.X, ALU.max, apply_absolute_value=True), sem="gd")
                        e.wait("vector", "gd")
                        e.op("vector", lambda g, t=t: g.tensor_tensor(
                            hmax_a[:, t:t + 1], hmax_a[:, t:t + 1],
                            tmpc[:, 6:7], ALU.max), sem="gd")
                        e.wait("gpsimd", "gd", 3 * n + 1)
                        e.dma("gpsimd", "ge" if n % 2 == 0 else "gy",
                              h_sc[a * ASC + 128 * t:a * ASC + 128 * t + 128,
                                   c0:c0 + cw],
                              hbuf[:, NW * (2 + t % 2):NW * (2 + t % 2) + cw])
                e.barrier()
                e.op("vector", lambda g: g.tensor_scalar_max(
                    ch_a[:, 0:AT], hmax_a[:, 0:AT], EPS), sem="lv")
                e.wait("vector", "lv")
                e.op("vector", lambda g: g.tensor_scalar_mul(
                    hs_a[:, 0:AT], ch_a[:, 0:AT], 0.5 / 127.0), sem="lv")
                e.wait("vector", "lv")
                e.op("vector", lambda g: g.reciprocal(ch_a[:, 0:AT], ch_a[:, 0:AT]),
                     sem="lv")
                e.wait("vector", "lv")
                e.op("vector", lambda g: g.tensor_scalar_mul(
                    ch_a[:, 0:AT], ch_a[:, 0:AT], 127.0), sem="lv")
                e.barrier()
                NQ = BT * NFC
                ND = len(DCH) * NP
                hqs = [big_b[:, 0:FT * BSC].rearrange(
                           "p (f t) -> p f t", f=FT),
                       big_a[:, 0:FT * BSC].rearrange(
                           "p (f t) -> p f t", f=FT)]
                NSC = ASC // BSC

                def quant_group(k, tb, ci):
                    b0 = a * ASC + k * BSC
                    tA0 = k * BT
                    hql = hqs[k % 2]
                    c0, cw = FCH[ci]
                    q = k * NQ + tb * NFC + ci
                    sl = q % 4
                    e.wait("sync", "hc", q - 3)
                    lsem = ["ha", "ha2", "qa", "qa2"][q % 4]
                    e.dma("sync", lsem, hbuf[:, NW * sl:NW * sl + cw],
                          h_sc[b0 + 128 * tb:b0 + 128 * tb + 128,
                               c0:c0 + cw])
                    e.wait("scalar", lsem, 16 * (q // 4 + 1))
                    e.op("scalar", lambda g, sl=sl, cw=cw, tl=tA0 + tb:
                         g.activation(
                             hbuf[:, NW * sl:NW * sl + cw],
                             hbuf[:, NW * sl:NW * sl + cw], AF.Relu,
                             scale=ch_a[:, tl:tl + 1],
                             bias=epsc[:, 2:3]), sem="hb")
                    e.wait("tensor", "hb", q + 1)
                    e.wait("tensor", "hd", 4 * (q - 4) + 2)
                    nblk = (cw + 127) // 128
                    pb = [0, 1, 6, 7][q % 4]
                    for b2 in range(nblk):
                        e.op("tensor", lambda g, sl=sl, b2=b2, pb=pb:
                             g.transpose(
                                 ps[pb][:, 128 * b2:128 * b2 + 128],
                                 hbuf[:, NW * sl + 128 * b2:
                                      NW * sl + 128 * b2 + 128],
                                 identf[:]),
                             sem=("hc" if b2 == nblk - 1 else None))
                    e.wait("vector", "hc", q + 1)
                    e.wait("vector", "hd", 4 * q)
                    e.op("vector", lambda g, pb=pb, nblk=nblk:
                         g.tensor_single_scalar(
                             i32b[:, 0:128 * nblk], ps[pb][:, 0:128 * nblk],
                             MAGIC, ALU.subtract), sem="hd")
                    e.wait("vector", "hd")
                    e.op("vector", lambda g, pb=pb, nblk=nblk:
                         g.scalar_tensor_tensor(
                             obuf[:, 0:128 * nblk], ps[pb][:, 0:128 * nblk],
                             -MAGIC, i32b[:, 0:128 * nblk],
                             ALU.add, ALU.subtract), sem="hd")
                    e.wait("vector", "hd")
                    e.op("vector", lambda g, nblk=nblk:
                         g.tensor_single_scalar(
                             i32c[:, 0:128 * nblk], obuf[:, 0:128 * nblk],
                             2.0, ALU.mult), sem="hd")
                    e.wait("vector", "hd")
                    e.op("vector", lambda g, ci=ci, nblk=nblk, tb=tb, hql=hql:
                         g.tensor_add(
                             hql[:, (NW // 128) * ci:
                                 (NW // 128) * ci + nblk,
                                 128 * tb:128 * tb + 128],
                             i32b[:, 0:128 * nblk].rearrange(
                                 "p (f t) -> p f t", f=nblk),
                             i32c[:, 0:128 * nblk].rearrange(
                                 "p (f t) -> p f t", f=nblk)), sem="hd")

                def down_part(j, di, pj):
                    b0 = a * ASC + j * BSC
                    tA0 = j * BT
                    hql = hqs[j % 2]
                    d0, dw = DCH[di]
                    f0, fn = W3[pj]
                    part = (j * len(DCH) + di) * NP + pj
                    wsl3 = part % 2
                    e.wait("sync", "hf", part - 1)
                    lsem = "he" if wsl3 == 0 else "he2"
                    for f1 in range(0, fn, 4):
                        f2 = min(fn, f1 + 4)
                        e.dma("sync", lsem,
                              wbuf[:, G3 * NW * wsl3 + NW * f1:
                                   G3 * NW * wsl3 + NW * f2]
                              .rearrange("p (f c) -> p f c", f=f2 - f1)
                              [:, :, 0:dw],
                              wdq_sc[128 * (f0 + f1):128 * (f0 + f2),
                                     d0:d0 + dw]
                              .rearrange("(f p) c -> p f c", f=f2 - f1))
                    e.wait("tensor", lsem)
                    if di == 0 and pj == 0:
                        e.wait("tensor", "hd", 4 * (j + 1) * NQ)
                    for tb in range(BT):
                        pb = 2 + tb
                        if pj == 0:
                            e.wait("tensor", "hg",
                                   (j * len(DCH) + di - 1) * BT + tb + 1)
                        for fi in range(fn):
                            e.op("tensor",
                                 lambda g, pb=pb, f0=f0, fi=fi, tb=tb,
                                 wsl3=wsl3, dw=dw, hql=hql:
                                 g.matmul(
                                     ps[pb][:, 0:dw],
                                     hql[:, f0 + fi,
                                         128 * tb:128 * tb + 128],
                                     wbuf[:, G3 * NW * wsl3 + NW * fi:
                                          G3 * NW * wsl3 + NW * fi + dw],
                                     start=(f0 + fi == 0),
                                     stop=(f0 + fi == FT - 1)),
                                 sem=("hf" if (fi == fn - 1 and
                                               tb == BT - 1) else None))
                    if pj != NP - 1:
                        return
                    for tb in range(BT):
                        pb = 2 + tb
                        tg = (b0 + 128 * tb) // 128
                        idx = (j * len(DCH) + di) * BT + tb
                        sl = idx % 2
                        ssem = "ge" if sl == 0 else "gy"
                        e.wait("sync", "hg", idx - 1)
                        lsem = "ls" if sl == 0 else "ls2"
                        e.dma("sync", lsem,
                              xbuf[:, D * sl + d0:D * sl + d0 + dw],
                              resid_tile_fn(tg)[:, d0:d0 + dw])
                        e.wait("vector", lsem, 16 * (idx // 2 + 1))
                        e.wait("vector", "hf", (j * len(DCH) + di + 1) * NP)
                        e.wait("vector", ssem, 16 * (idx // 2))
                        e.op("vector", lambda g, pb=pb, sl=sl, d0=d0, dw=dw,
                             tl=tA0 + tb:
                             g.scalar_tensor_tensor(
                                 obuf[:, 512 + NW * sl:512 + NW * sl + dw],
                                 ps[pb][:, 0:dw], hs_a[:, tl:tl + 1],
                                 xbuf[:, D * sl + d0:D * sl + d0 + dw],
                                 ALU.mult, ALU.add), sem="hg")
                        e.wait("gpsimd", "hg", idx + 1)
                        e.dma("gpsimd", ssem,
                              out_tile_fn(tg)[:, d0:d0 + dw],
                              obuf[:, 512 + NW * sl:512 + NW * sl + dw])

                for k in range(NSC):
                    qgs = [(k, tb, ci) for tb in range(BT)
                           for ci in range(NFC)]
                    dps = ([(k - 1, di, pj) for di in range(len(DCH))
                            for pj in range(NP)] if k >= 1 else [])
                    e.wait("vector", "hf", ND * (k - 1))
                    di_ = 0
                    for i, (kk, tb, ci) in enumerate(qgs):
                        quant_group(kk, tb, ci)
                        while dps and di_ * len(qgs) <= (i + 1) * len(dps):
                            down_part(*dps.pop(0))
                            di_ += 1
                    for dp in dps:
                        down_part(*dp)
                for dp in [(NSC - 1, di, pj) for di in range(len(DCH))
                           for pj in range(NP)]:
                    down_part(*dp)
                e.barrier()

        def p3_qkv(e: E):
            AT = ASC // 128
            xlv = big_a[:, 0:DT * ASC].rearrange("p (d t) -> p d t", d=DT)
            e.op("vector", lambda g: g.memset(srow[0:1, 16:112], 0.0), sem="af")
            for a in range(TA // ASC):
                is_own = (a * ASC) >= TC
                ln_quant_sweep(e, a, AT,
                               lambda tg: x2_sc[128 * tg:128 * tg + 128, :],
                               want_quant=False)
                e.barrier()
                fstart = 0 if is_own else QF
                fchunks = [("qk", f, min(NW, QF + KF - f))
                           for f in range(fstart, QF + KF, NW)]
                vchunks = [("v", QF + KF + i * NW, min(NW, VF - i * NW))
                           for i in range((VF + NW - 1) // NW)]
                groups_per = {}
                qb_hist = []
                for kind, f0, fw in fchunks + vchunks:
                    ngr = (fw // 128) * (ASC // NW) if kind == "qk" else AT
                    qb_hist.append(ngr)
                norm_pend = []

                def flush_norm(e=e):
                    if not norm_pend:
                        return
                    m, col = norm_pend.pop()
                    e.wait("tensor", "ab", m + 1)
                    e.wait("tensor", "af", 2 * (m - 1))
                    e.op("tensor", lambda g, m=m: g.matmul(
                        ps[4 + m % 2][0:1, 0:NW], onescf[:],
                        hbuf[:, NW * (m % 2):NW * (m % 2) + NW],
                        start=True, stop=True), sem="ac")
                    e.wait("vector", "ac", m + 1)
                    e.wait("vector", "af", 2 * m)
                    e.op("vector", lambda g, m=m: g.reduce_max(
                        tmpc[0:1, 0:1], ps[4 + m % 2][0:1, 0:NW], AX.X),
                        sem="af")
                    e.wait("vector", "af")
                    e.op("vector", lambda g, col=col: g.tensor_tensor(
                        srow[0:1, col:col + 1], srow[0:1, col:col + 1],
                        tmpc[0:1, 0:1], ALU.max), sem="af")
                for cj, (kind, f0, fw) in enumerate(fchunks + vchunks):
                    wsl = cj % 2
                    e.wait("sync", "qb",
                           sum(qb_hist[:cj - 1]) if cj >= 2 else 0)
                    lsem = "qa" if wsl == 0 else "qa2"
                    DQ3 = max(1, DT // 4)
                    for d1 in range(0, DT, DQ3):
                        e.dma("sync", lsem,
                              wbuf[:, DT * NW * wsl + NW * d1:
                                   DT * NW * wsl + NW * (d1 + DQ3)]
                              .rearrange("p (d c) -> p d c", d=DQ3)
                              [:, :, 0:fw],
                              wqkv_bsc[128 * d1:128 * (d1 + DQ3), f0:f0 + fw]
                              .rearrange("(d p) c -> p d c", d=DQ3))
                    e.wait("tensor", lsem)
                    if kind == "qk":
                        it = [(fi, tc) for fi in range(fw // 128)
                              for tc in range(ASC // NW)]
                    else:
                        it = [(0, t) for t in range(AT)]
                    for (fi, tc) in it:
                        gidx = e.loc("qb")
                        pb = 2 + gidx % 2
                        e.wait("tensor", "qc", gidx - 1)
                        if kind == "qk":
                            for dti in range(DT):
                                e.op("tensor",
                                     lambda g, pb=pb, wsl=wsl, fi=fi, dti=dti, tc=tc:
                                     g.matmul(
                                         ps[pb][:, 0:NW],
                                         wbuf[:, DT * NW * wsl + NW * dti + 128 * fi:
                                              DT * NW * wsl + NW * dti + 128 * fi + 128],
                                         xlv[:, dti, NW * tc:NW * tc + NW],
                                         start=(dti == 0), stop=False))
                            e.op("tensor", lambda g, pb=pb, f0=f0, fi=fi:
                                 g.matmul(ps[pb][:, 0:NW],
                                          bqkvrow[0:1, f0 + 128 * fi:
                                                  f0 + 128 * fi + 128],
                                          onesrow[0:1, 0:NW],
                                          start=False, stop=True), sem="qb")
                        else:
                            t = tc
                            for dti in range(DT):
                                e.op("tensor",
                                     lambda g, pb=pb, wsl=wsl, dti=dti, t=t, fw=fw:
                                     g.matmul(
                                         ps[pb][:, 0:fw],
                                         xlv[:, dti, 128 * t:128 * t + 128],
                                         wbuf[:, DT * NW * wsl + NW * dti:
                                              DT * NW * wsl + NW * dti + fw],
                                         start=(dti == 0), stop=False))
                            e.op("tensor", lambda g, pb=pb, f0=f0, fw=fw:
                                 g.matmul(ps[pb][:, 0:fw], ones1[:],
                                          bqkvrow[0:1, f0:f0 + fw],
                                          start=False, stop=True), sem="qb")
                        flush_norm()
                        e.wait("scalar", "qb")
                        e.wait("scalar", "qd" if gidx % 2 == 0 else "qe",
                               16 * (gidx // 2))
                        esl = gidx % 2
                        ew = NW if kind == "qk" else fw
                        e.op("scalar", lambda g, pb=pb, esl=esl, ew=ew: g.activation(
                            ebuf[:, NW * esl:NW * esl + ew], ps[pb][:, 0:ew],
                            AF.Copy), sem="qc")
                        if kind == "qk":
                            m = e.loc("ab")
                            f = f0 + 128 * fi
                            col = (16 + f // 128) if f < QF else (32 + (f - QF) // 128)
                            e.wait("scalar", "ac", m - 1)
                            e.op("scalar", lambda g, m=m, esl=esl: g.activation(
                                hbuf[:, NW * (m % 2):NW * (m % 2) + NW],
                                ebuf[:, NW * esl:NW * esl + NW],
                                AF.Square), sem="ab")
                            norm_pend.append((m, col))
                        e.wait("gpsimd", "qc", gidx + 1)
                        qsem = "qd" if gidx % 2 == 0 else "qe"
                        if kind == "qk":
                            e.dma("gpsimd", qsem,
                                  qk_sc[f0 + 128 * fi:f0 + 128 * fi + 128,
                                        a * ASC + NW * tc:a * ASC + NW * tc + NW],
                                  ebuf[:, NW * esl:NW * esl + NW])
                        else:
                            e.dma("gpsimd", qsem,
                                  v_sc[a * ASC + 128 * tc:a * ASC + 128 * tc + 128,
                                       f0 - QF - KF:f0 - QF - KF + fw],
                                  ebuf[:, NW * esl:NW * esl + fw])
                flush_norm()
                e.barrier()

        def p4_attn(e: E):
            QCN = TO // QC
            ofv = big_a[:, 0:2 * PH * TO].rearrange("p (f t) -> p f t", f=2 * PH)
            kb0, qb0, vb0 = 0, 2 * TA, 2 * TA + 2 * TO
            vext_v = big_b[:, vb0:vb0 + KT * VEXT].rearrange(
                "p (k c) -> p k c", k=KT)
            e.op("vector", lambda g: g.memset(vext_v[:, :, 256:257], 1.0),
                 sem="af")
            for kv in range(KVPH):
                for r in range(2):
                    e.op("vector", lambda g, kv=kv, r=r: g.tensor_copy(
                        srow[0:1, 48 + 4 * kv + 2 * r:50 + 4 * kv + 2 * r],
                        srow[0:1, 32 + 2 * kv:34 + 2 * kv]),
                        sem=("af" if (kv == KVPH - 1 and r == 1) else None))
            e.wait("vector", "af")
            e.op("vector", lambda g: g.tensor_mul(
                srow[0:1, 64:64 + 2 * PH], srow[0:1, 16:16 + 2 * PH],
                srow[0:1, 48:48 + 2 * PH]), sem="af")
            e.wait("scalar", "af")
            e.op("scalar", lambda g: g.activation(
                srow[0:1, 64:64 + 2 * PH], srow[0:1, 64:64 + 2 * PH], AF.Ln,
                bias=epsc[0:1, 1:2]), sem="oa")
            e.wait("scalar", "oa")
            e.op("scalar", lambda g: g.activation(
                srow[0:1, 64:64 + 2 * PH], srow[0:1, 64:64 + 2 * PH], AF.Exp,
                scale=0.5), sem="oa")
            e.wait("vector", "oa")
            e.op("vector", lambda g: g.tensor_scalar_mul(
                srow[0:1, 64:64 + 2 * PH], srow[0:1, 64:64 + 2 * PH],
                -ALPHA), sem="af")
            e.wait("vector", "af")
            e.op("vector", lambda g: g.tensor_scalar(
                srow[0:1, 80:80 + 2 * PH], srow[0:1, 64:64 + 2 * PH],
                srow[0:1, 8:9], None, ALU.add), sem="af")
            e.wait("vector", "af")
            e.op("vector", lambda g: g.tensor_copy(
                ebuf[0:1, 32:64], srow[0:1, 64:96]), sem="af")
            e.wait("tensor", "af")
            e.op("tensor", lambda g: g.matmul(
                ps[6][:, 0:32], ones1[:], ebuf[0:1, 32:64],
                start=True, stop=True), sem="ac")
            e.wait("scalar", "ac")
            e.op("scalar", lambda g: g.activation(
                mtab[:, 0:32], ps[6][:, 0:32], AF.Copy), sem="oa")
            e.barrier()
            SBKS = [0, 1, 6]
            for ph in range(PH):
                kvph = ph // 2
                if ph % 2 == 0:
                    for c in range(2):
                        e.dma("sync", "aa",
                              big_b[:, kb0 + TA * c:kb0 + TA * c + TA],
                              qk_sc[QF + kvph * 256 + 128 * c:
                                    QF + kvph * 256 + 128 * c + 128, :])
                    KQ = max(1, KT // 4)
                    for k1 in range(0, KT, KQ):
                        e.dma("sync", "aa", vext_v[:, k1:k1 + KQ, 0:256],
                              v_sc[128 * k1:128 * (k1 + KQ),
                                   kvph * 256:kvph * 256 + 256]
                              .rearrange("(k p) c -> p k c", k=KQ))
                for c in range(2):
                    e.dma("sync", "aa",
                          big_b[:, qb0 + TO * c:qb0 + TO * c + TO],
                          qk_sc[ph * 256 + 128 * c:ph * 256 + 128 * c + 128, TC:])
                e.wait("tensor", "aa")
                for qc in range(QCN):
                    allb = []
                    for comp in range(2):
                        ctx_kts = list(range(TC // 128))
                        own_full = [TC // 128 + j for j in range(qc * QS)]
                        band_kts = [TC // 128 + qc * QS + j for j in range(QS)]
                        bs = []
                        for i in range(0, len(ctx_kts), 2):
                            bs.append((comp, ctx_kts[i:i + 2], True, ()))
                        for i in range(0, len(own_full), 2):
                            bs.append((comp, own_full[i:i + 2], False, ()))
                        for kt in band_kts:
                            bs.append((comp, [kt], False, (kt,)))
                        allb += [bi + (j == 0, j == len(bs) - 1)
                                 for j, bi in enumerate(bs)]
                    NB = len(allb)
                    ads = []
                    afb = {}

                    def emit_scores(n, e=e, allb=allb, ads=ads):
                        comp, kts_b, is_ctx, bands, first, last = allb[n]
                        g0 = e.loc("ad")
                        sbk = SBKS[g0 % 3]
                        e.wait("tensor", "ae", 2 * (g0 - 2))
                        nb = len(kts_b)
                        for bi_k, kt in enumerate(kts_b):
                            e.op("tensor",
                                 lambda g, sbk=sbk, comp=comp, kt=kt, qc=qc,
                                 bi_k=bi_k:
                                 g.matmul(
                                     ps[sbk][:, QC * bi_k:QC * bi_k + QC],
                                     big_b[:, kb0 + TA * comp + 128 * kt:
                                          kb0 + TA * comp + 128 * kt + 128],
                                     big_b[:, qb0 + TO * comp + QC * qc:
                                          qb0 + TO * comp + QC * qc + QC],
                                     start=True, stop=True),
                                 sem=("ad" if bi_k == nb - 1 else None))
                        ads.append(e.loc("ad"))
                    emit_scores(0)
                    for n in range(NB):
                        comp, kts_b, is_ctx, bands, first, last = allb[n]
                        if n + 1 < NB:
                            emit_scores(n + 1)
                        nb = len(kts_b)
                        sbk = SBKS[(ads[n] - 1) % 3]
                        for kt in bands:
                            dj = kt - TC // 128 - qc * QS
                            e.wait("vector", "ad", ads[n])
                            e.op("vector", lambda g, sbk=sbk, dj=dj:
                                 g.tensor_add(ps[sbk][:, 0:QC],
                                              ps[sbk][:, 0:QC],
                                              bandm[:, QC * dj:QC * dj + QC]),
                                 sem="af")
                            afb[n] = e.loc("af")
                        if n in afb:
                            e.wait("scalar", "af", afb[n])
                        else:
                            e.wait("scalar", "ad", ads[n])
                        bcol = (2 * ph + comp) + (16 if is_ctx else 0)
                        i_sc = e.loc("ae") // 2
                        esl = i_sc % 4
                        e.wait("scalar", "aq", i_sc - 3)
                        for qs in range(QS):
                            e.op("scalar", lambda g, sbk=sbk, esl=esl,
                                 bcol=bcol, nb=nb, qs=qs:
                                 g.activation(
                                     ebuf[:, 512 * esl:512 * esl + QC * nb]
                                     .rearrange("p (b q) -> p b q", b=nb)
                                     [:, :, 128 * qs:128 * qs + 128],
                                     ps[sbk][:, 0:QC * nb]
                                     .rearrange("p (b q) -> p b q", b=nb)
                                     [:, :, 128 * qs:128 * qs + 128],
                                     AF.Exp, scale=ALPHA,
                                     bias=mtab[:, bcol:bcol + 1]), sem="ae")
                            e.wait("tensor", "ae")
                            pb = 2 + 2 * comp + qs
                            if first:
                                e.wait("tensor", "ao",
                                       (qc - 1) * QS + qs + 1)
                            for bi_k, kt in enumerate(kts_b):
                                e.op("tensor",
                                     lambda g, pb=pb, esl=esl, qs=qs, kt=kt,
                                     bi_k=bi_k, first=first, last=last, nb=nb:
                                     g.matmul(
                                         ps[pb][:, 0:VEXT],
                                         ebuf[:, 512 * esl + QC * bi_k + 128 * qs:
                                              512 * esl + QC * bi_k + 128 * qs + 128],
                                         vext_v[:, kt, :],
                                         start=(first and bi_k == 0),
                                         stop=(last and bi_k == nb - 1)),
                                     sem=("aq" if (qs == QS - 1 and
                                                   bi_k == nb - 1) else None))
                    e.wait("vector", "aq", e.loc("ae") // 2)
                    for qs in range(QS):
                        pb1, pb2 = 2 + qs, 4 + qs
                        osl = 512 * (qc % 2) + 256 * qs
                        sqs = 1024 + 256 * qs
                        e.op("vector", lambda g, pb1=pb1: g.reciprocal(
                            tmpc[:, 0:1], ps[pb1][:, 256:257]))
                        e.op("vector", lambda g, pb2=pb2: g.reciprocal(
                            tmpc[:, 1:2], ps[pb2][:, 256:257]), sem="af")
                        e.wait("vector", "af")
                        e.op("vector", lambda g: g.tensor_mul(
                            tmpc[:, 2:3], tmpc[:, 1:2], lamncol[:]), sem="af")
                        e.wait("vector", "af")
                        e.wait("vector", "ab", 4 * (qc - 1))
                        e.op("vector", lambda g, pb1=pb1, osl=osl: g.tensor_scalar(
                            obuf[:, osl:osl + 256], ps[pb1][:, 0:256],
                            tmpc[:, 0:1], None, ALU.mult), sem="af")
                        e.wait("vector", "af")
                        e.op("vector", lambda g, pb2=pb2, osl=osl:
                             g.scalar_tensor_tensor(
                                 obuf[:, osl:osl + 256], ps[pb2][:, 0:256],
                                 tmpc[:, 2:3], obuf[:, osl:osl + 256],
                                 ALU.mult, ALU.add), sem="ao")
                        e.wait("scalar", "ao")
                        e.op("scalar", lambda g, osl=osl, sqs=sqs: g.activation(
                            obuf[:, sqs:sqs + 256], obuf[:, osl:osl + 256],
                            AF.Square), sem="oa")
                        e.wait("vector", "oa")
                        e.op("vector", lambda g, sqs=sqs: g.reduce_sum(
                            tmpc[:, 3:4], obuf[:, sqs:sqs + 256], AX.X),
                            sem="af")
                        e.wait("scalar", "af")
                        e.op("scalar", lambda g: g.activation(
                            tmpc[:, 4:5], tmpc[:, 3:4], AF.Ln,
                            scale=1.0 / 256.0, bias=epsc[:, 0:1]), sem="oa")
                        e.wait("scalar", "oa")
                        e.op("scalar", lambda g: g.activation(
                            tmpc[:, 4:5], tmpc[:, 4:5], AF.Exp, scale=-0.5),
                            sem="oa")
                        e.wait("vector", "oa")
                        e.op("vector", lambda g, osl=osl: g.scalar_tensor_tensor(
                            obuf[:, osl:osl + 256], obuf[:, osl:osl + 256],
                            tmpc[:, 4:5], swtile[:], ALU.mult, ALU.mult),
                            sem="ac")

                    def emit_tevac(qcm, e=e, ph=ph):
                        e.wait("tensor", "ac", 2 * qcm + 2)
                        e.wait("tensor", "ag", 2 * qcm)
                        for qs in range(QS):
                            for eh in range(2):
                                e.op("tensor", lambda g, qcm=qcm, qs=qs, eh=eh:
                                     g.transpose(
                                         ps[7][:, 128 * (2 * eh + qs):
                                               128 * (2 * eh + qs) + 128],
                                         obuf[:, 512 * (qcm % 2) + 256 * qs +
                                              128 * eh:
                                              512 * (qcm % 2) + 256 * qs +
                                              128 * eh + 128],
                                         identf[:]), sem="ab")
                        e.wait("scalar", "ab")
                        for eh in range(2):
                            e.op("scalar", lambda g, eh=eh, qcm=qcm, ph=ph:
                                 g.activation(
                                     ofv[:, 2 * ph + eh,
                                         256 * qcm:256 * qcm + 256],
                                     ps[7][:, 256 * eh:256 * eh + 256],
                                     AF.Copy), sem="ag")
                    if qc > 0:
                        emit_tevac(qc - 1)
                emit_tevac(QCN - 1)
                e.barrier()

        def p5_outproj(e: E):
            ofv = big_a[:, 0:2 * PH * TO].rearrange("p (f t) -> p f t", f=2 * PH)
            OFT = 2 * PH
            NT = TO // 128
            for di, (d0, dw) in enumerate(DCH):
                wsl = di % 2
                e.wait("sync", "oa", (di - 1) * NT)
                lsem = "ob" if wsl == 0 else "ob2"
                for fi in range(OFT):
                    e.dma("sync", lsem,
                          wbuf[:, OFT * NW * wsl + NW * fi:
                               OFT * NW * wsl + NW * fi + dw],
                          woT_bsc[128 * fi:128 * fi + 128, d0:d0 + dw])
                e.wait("tensor", lsem)
                for t in range(NT):
                    gidx = di * NT + t
                    pb = gidx % 2
                    e.wait("tensor", "oc", gidx - 1)
                    for fi in range(OFT):
                        e.op("tensor", lambda g, pb=pb, wsl=wsl, fi=fi, t=t, dw=dw:
                             g.matmul(ps[pb][:, 0:dw],
                                      ofv[:, fi, 128 * t:128 * t + 128],
                                      wbuf[:, OFT * NW * wsl + NW * fi:
                                           OFT * NW * wsl + NW * fi + dw],
                                      start=(fi == 0), stop=False))
                    e.op("tensor", lambda g, pb=pb, d0=d0, dw=dw:
                         g.matmul(ps[pb][:, 0:dw], ones1[:],
                                  borow[0:1, d0:d0 + dw],
                                  start=False, stop=True), sem="oa")
                    sl = gidx % 2
                    e.wait("sync", "oc", gidx - 1)
                    lsem = "od" if sl == 0 else "od2"
                    e.dma("sync", lsem, xbuf[:, D * sl + d0:D * sl + d0 + dw],
                          x2_sc[TC + 128 * t:TC + 128 * t + 128, d0:d0 + dw])
                    e.wait("vector", lsem, 16 * (gidx // 2 + 1))
                    e.wait("vector", "oa", gidx + 1)
                    e.wait("vector", "qd" if gidx % 2 == 0 else "qe",
                           16 * (gidx // 2))
                    e.op("vector", lambda g, pb=pb, sl=sl, d0=d0, dw=dw:
                         g.tensor_add(hbuf[:, NW * sl:NW * sl + dw],
                                      ps[pb][:, 0:dw],
                                      xbuf[:, D * sl + d0:D * sl + d0 + dw]),
                         sem="oc")
                    e.wait("gpsimd", "oc", gidx + 1)
                    e.dma("gpsimd", "qd" if gidx % 2 == 0 else "qe",
                          x3_sc[128 * t:128 * t + 128, d0:d0 + dw],
                          hbuf[:, NW * sl:NW * sl + dw])
            e.barrier()

        with nc.Block() as block:
            for me in ENGS:
                def emit(eng, me=me):
                    dbg_slot[0] = 0
                    e = E(me, eng, sems)
                    upto = cfg.get("upto", "all")
                    p0_consts(e)
                    p1_weights(e)
                    ffn_phase(e, TA, x_in_tile, x_in_tile,
                              lambda t: x2_sc[128 * t:128 * t + 128, :])
                    if upto != "ffn1":
                        p3_qkv(e)
                    if upto not in ("ffn1", "qkv"):
                        p4_attn(e)
                        p5_outproj(e)
                        ffn_phase(e, TO,
                                  lambda t: x3_sc[128 * t:128 * t + 128, :],
                                  lambda t: x3_sc[128 * t:128 * t + 128, :],
                                  lambda t: out_d[128 * t:128 * t + 128, :])
                    if cfg.get("dump"):
                        for nm, srct in [("o_x2", x2_sc), ("o_qk", qk_sc),
                                         ("o_v", v_sc), ("o_x3", x3_sc),
                                         ("o_h", h_sc), ("o_dxq", dbg_xq),
                                         ("o_dst", dbg_st)]:
                            R = srct.tensor.shape[0] if hasattr(srct, "tensor") else 0
                            nrow = dump[nm].tensor.shape[0]
                            for r0 in range(0, nrow, 1024):
                                rr = min(1024, nrow - r0)
                                e.dma("gpsimd", "qd", dump[nm][r0:r0 + rr, :],
                                      srct[r0:r0 + rr, :])
                        e.barrier()
                    if me == "gpsimd":
                        eng.wait_ge(sems["ge"], e.cnt["ge"])
                        eng.wait_ge(sems["gy"], e.cnt["gy"])
                getattr(block, me)(emit)

    if SPLIT:
        nc.has_collectives = True
    return nc



_BUILT = {}


def _get_nc(key, cfg):
    if key not in _BUILT:
        _BUILT[key] = build(cfg)
    return _BUILT[key]


def make_in_maps(inputs, cfg):
    D, TO, TC = cfg["D"], cfg["TO"], cfg["TC"]
    QC = cfg["QC"]
    QS = QC // 128
    x = np.ascontiguousarray(np.asarray(inputs["L0_attn_hidden_states"],
                                        np.float32))
    wqkvT = np.ascontiguousarray(np.asarray(inputs["L0_attn_Wqkv_w"],
                                            np.float32).T)
    woT = np.ascontiguousarray(np.asarray(inputs["L0_attn_out_w"],
                                          np.float32).T)
    lrows = np.ascontiguousarray(np.stack([
        np.asarray(inputs["L0_attn_lambda_q1"], np.float32),
        np.asarray(inputs["L0_attn_lambda_k1"], np.float32),
        np.asarray(inputs["L0_attn_lambda_q2"], np.float32),
        np.asarray(inputs["L0_attn_lambda_k2"], np.float32)]))
    ident = np.eye(128, dtype=np.float32)
    bm = np.zeros((QS * 128, QC), np.float32)
    kl = np.arange(128)[:, None]
    ql = np.arange(QC)[None, :]
    for dj in range(QS):
        bm[128 * dj:128 * dj + 128] = np.where(kl <= ql - 128 * dj, 0.0,
                                               np.float32(NEG))
    zeros_ctx = np.zeros((TC, D), np.float32)
    wg_f = np.ascontiguousarray(np.asarray(inputs["L0_ffn_w_gate"], np.float32))
    wu_f = np.ascontiguousarray(np.asarray(inputs["L0_ffn_w_up"], np.float32))
    wd_f = np.ascontiguousarray(np.asarray(inputs["L0_ffn_w_down"], np.float32))
    shared = {
        "wqkvT": wqkvT, "woT": woT,
        "bqkv": np.asarray(inputs["L0_attn_Wqkv_b"], np.float32).reshape(1, -1),
        "bo": np.asarray(inputs["L0_attn_out_b"], np.float32).reshape(1, -1),
        "lrows_in": lrows,
        "sublnw_in": np.asarray(inputs["L0_attn_subln_w"], np.float32).reshape(1, -1),
        "identf_in": ident, "bandmask_in": bm,
    }
    SPLIT = bool(cfg.get("split"))
    F = cfg["F"]
    FT, DT = F // 128, D // 128
    KWG, KWD = (FT + 7) // 8, DT // 8

    def row_slices(w, rjs):
        out = np.zeros((len(rjs) * 128, w.shape[1]), np.float32)
        for i, rj in enumerate(rjs):
            if 128 * rj < w.shape[0]:
                out[128 * i:128 * i + 128] = w[128 * rj:128 * rj + 128]
        return out
    in_maps = []
    for c in range(8):
        b, hh = c // 2, c % 2
        m = dict(shared)
        if SPLIT:
            m["wg_slc"] = row_slices(wg_f, [c + 8 * k for k in range(KWG)])
            m["wu_slc"] = row_slices(wu_f, [c + 8 * k for k in range(KWG)])
            m["wd_slc"] = row_slices(wd_f, [2 * c + k for k in range(KWD)])
        else:
            m["wg"], m["wu"], m["wd"] = wg_f, wu_f, wd_f
        m["x_own"] = np.ascontiguousarray(x[b, TO * hh:TO * hh + TO])
        m["x_ctx"] = (np.ascontiguousarray(x[b, 0:TC]) if hh == 1 else zeros_ctx)
        m["ctxbias"] = np.array([[0.0 if hh == 1 else NEG]], np.float32)
        in_maps.append(m)
    return in_maps


def kernel(**inputs):
    cfg = dict(FULL)
    nc = _get_nc("full", cfg)
    in_maps = make_in_maps(inputs, cfg)
    res = run_bass_kernel_spmd(nc, in_maps, core_ids=list(range(8)))
    x = np.asarray(inputs["L0_attn_hidden_states"])
    B, S, D = x.shape
    out = np.empty((B, S, D), np.float32)
    for c in range(8):
        b, hh = c // 2, c % 2
        out[b, cfg["TO"] * hh:cfg["TO"] * hh + cfg["TO"]] = \
            res.results[c]["out"]
    return out
